# version 03
# baseline (speedup 1.0000x reference)
import numpy as np
import ml_dtypes
import concourse.bass as bass
import concourse.mybir as mybir
from concourse.bass_utils import run_bass_kernel_spmd

F32 = mybir.dt.float32
F32R = mybir.dt.float32r
BF16 = mybir.dt.bfloat16
I8 = mybir.dt.int8
AF = mybir.ActivationFunctionType
ALU = mybir.AluOpType
AX = mybir.AxisListType

S, D, II = 4096, 1024, 4096
T = S // 128
NC = 8
EPS = 1e-5
ENGS = ("sync", "scalar", "vector", "tensor", "gpsimd")


class Prog:
    def __init__(self):
        self.ops = {e: [] for e in ENGS}
        self.semv = {s: 0 for s in ("sq", "lq", "gq", "pe", "act", "dve", "gp")}
        self.waited = {e: {} for e in ENGS}
        self.lastw = {e: frozenset() for e in ENGS}
        self.dummy = {}
        self.n_sep = 0

    def wait(self, eng, tok):
        if tok is None:
            return
        nm, v = tok
        if v <= 0 or self.waited[eng].get(nm, -1) >= v:
            return
        self.ops[eng].append(("w", nm, v))
        self.waited[eng][nm] = v

    def op(self, eng, fn, reads=(), writes=(), waits=(), inc=None, inck=1):
        for tk in waits:
            if isinstance(tk, list):
                for t2 in tk:
                    self.wait(eng, t2)
            else:
                self.wait(eng, tk)
        if frozenset(reads) & self.lastw[eng]:
            df = self.dummy.get(eng)
            if df is not None:
                self.ops[eng].append(("o", df, None, 0))
                self.lastw[eng] = frozenset()
                self.n_sep += 1
        tok = None
        if inc is not None:
            self.semv[inc] += inck
            tok = (inc, self.semv[inc])
        self.ops[eng].append(("o", fn, inc, inck))
        self.lastw[eng] = frozenset(writes)
        return tok

    def barrier(self):
        toks = [(nm, v) for nm, v in self.semv.items()]
        for eng in ENGS:
            for tk in toks:
                self.wait(eng, tk)

    def replay(self, block, semh):
        for en in ENGS:
            def make(en):
                def _(engine):
                    for it in self.ops[en]:
                        if it[0] == "w":
                            engine.wait_ge(semh[it[1]], it[2])
                        else:
                            ins = it[1](engine)
                            if it[2] is not None:
                                ins.then_inc(semh[it[2]], it[3])
                return _
            getattr(block, en)(make(en))


def build(debug_pass1=False, debug_scr=False):
    nc = bass.Bass("TRN2", target_bir_lowering=False)

    x_h = nc.dram_tensor("x", [S, D], F32, kind="ExternalInput").ap()
    wh = {nm: nc.dram_tensor(nm, [D, D], F32, kind="ExternalInput").ap()
          for nm in ("wq", "wk", "wv", "wqa", "wka")}
    vr_h = {nm: nc.dram_tensor("v_" + nm, [1, D], F32, kind="ExternalInput").ap()
            for nm in ("bq", "bk", "bv", "bqa", "bka")}
    gcols_h = nc.dram_tensor("gcolsc", [128, 8], F32, kind="ExternalInput").ap()
    bcols_h = nc.dram_tensor("bcolsc", [128, 8], F32, kind="ExternalInput").ap()
    wf_h = {"g": nc.dram_tensor("wgate", [II, D], F32, kind="ExternalInput").ap(),
            "u": nc.dram_tensor("wup", [II, D], F32, kind="ExternalInput").ap(),
            "d": nc.dram_tensor("wdown", [D, II], F32, kind="ExternalInput").ap()}
    ident_h = nc.dram_tensor("identc", [128, 128], F32, kind="ExternalInput").ap()
    utri_h = nc.dram_tensor("utric", [128, 128], F32, kind="ExternalInput").ap()
    invs_h = nc.dram_tensor("invsc", [128, T], F32, kind="ExternalInput").ap()

    out_h = nc.dram_tensor("out", [S, D], F32, kind="ExternalOutput").ap()
    ysk = "ExternalOutput" if (debug_pass1 or debug_scr) else "Internal"
    wsk = "ExternalOutput" if debug_scr else "Internal"
    y_scr = nc.dram_tensor("y_scr", [S, D], F32, kind=ysk).ap()
    aq_scr = nc.dram_tensor("aq_scr", [S, D], BF16, kind=ysk).ap()
    wq_scr = {"g": nc.dram_tensor("wgq", [II, D], BF16, kind=wsk).ap(),
              "u": nc.dram_tensor("wuq", [II, D], BF16, kind=wsk).ap(),
              "d": nc.dram_tensor("wdq", [D, II], BF16, kind=wsk).ap()}
    aq2_scr = nc.dram_tensor("aq2", [S, II], BF16, kind=wsk).ap()

    p = Prog()

    BASE = (nc.sbuf_base + 63) // 64 * 64
    CAP = nc.sbuf_top - BASE
    off = [0]

    def place(name, shape, dtype, at=None, align=64):
        if at is None:
            off[0] = (off[0] + align - 1) // align * align
            at = off[0]
            off[0] += shape[1] * mybir.dt.size(dtype) if len(shape) == 2 else shape[0]
        th = nc.alloc_sbuf_tensor_at(name, shape, dtype, offset=BASE + at)
        return th, at

    sm, _ = place("sm", [128, 48], F32)
    wsm, _ = place("wsm", [128, 8], F32)
    inv_s1, _ = place("inv_s1", [128, T], F32)
    inv_s2, _ = place("inv_s2", [128, T], F32)
    pst, _ = place("pst", [128, 64], F32)
    allon_f, _ = place("allon_f", [128, 128], F32)
    G1 = (off[0] + 63) // 64 * 64

    off[0] = G1
    W = {}
    for nm in ("wq", "wk", "wqa", "wka"):
        W[nm], _ = place("W_" + nm, [128, 8 * D], F32R)
    W["wv"], _ = place("W_wv", [128, 8 * D], BF16)
    btile = {}
    for nm in ("bq", "bk", "bv"):
        btile[nm], _ = place("bt_" + nm, [128, D], BF16)
    rows_bf, _ = place("rows_bf", [1, 2 * D + 128], BF16)
    bqa_row = rows_bf[0:1, 0:D]
    bka_row = rows_bf[0:1, D:2 * D]
    ones_row = rows_bf[0:1, 2 * D:2 * D + 128]
    carryQ, _ = place("carryQ", [128, D], F32R)
    carryK, _ = place("carryK", [128, D], F32R)
    ident_r, _ = place("ident_r", [128, 128], F32R)
    utri_r, _ = place("utri_r", [128, 128], F32R)
    allon_r, _ = place("allon_r", [128, 128], F32R)
    invs, _ = place("invs", [128, T], F32)
    gcols, _ = place("gcolsb", [128, 8], F32)
    bcols_r, _ = place("bcols_r", [128, 8], F32R)
    bcols_bf, _ = place("bcols_bf", [128, 8], BF16)
    PH = (off[0] + 63) // 64 * 64

    off[0] = PH
    xbuf, _ = place("xbuf", [128, D], F32)
    xres, _ = place("xres", [128, D], F32)
    zdump, zoff = place("zdump", [128, D], F32)
    zr, _ = place("zr", [128, D], F32R, at=zoff)
    shfm, _ = place("shfm", [128, D], F32R)
    shbf, _ = place("shbf", [128, D], BF16)
    qbuf, _ = place("qbuf", [128, D], F32R)
    kbuf, koff = place("kbuf", [128, D], F32)
    kr, _ = place("kr", [128, D], F32R, at=koff)
    vbuf, _ = place("vbuf", [128, D], F32)
    abuf, aoff = place("abuf", [128, D], F32R)
    ybuf, _ = place("ybuf", [128, D], F32, at=aoff)
    aqbuf, _ = place("aqbuf", [128, D], I8)
    stagF = []
    for i in range(2):
        t_, _ = place(f"stagF{i}", [128, 512], F32)
        stagF.append(t_)
    qi8, _ = place("qi8", [128, 512], I8)
    assert off[0] <= CAP, f"pass1 overflow {off[0]} > {CAP}"

    po = [PH]

    def pplace(name, shape, dtype):
        o = (po[0] + 63) // 64 * 64
        th = nc.alloc_sbuf_tensor_at(name, shape, dtype, offset=BASE + o)
        po[0] = o + shape[1] * mybir.dt.size(dtype)
        return th

    vrow = {nm: pplace("vr_" + nm, [1, D], F32)
            for nm in ("bq", "bk", "bv", "bqa", "bka")}
    crow_bf = pplace("crow_bf", [1, D], BF16)
    stagA = pplace("stagA", [128, D], F32)
    stagB = pplace("stagB", [128, D], F32)
    cst = pplace("cst", [128, 400], F32)
    assert po[0] <= CAP

    o2 = [G1]

    def p2place(name, shape, dtype):
        o = (o2[0] + 63) // 64 * 64
        th = nc.alloc_sbuf_tensor_at(name, shape, dtype, offset=BASE + o)
        o2[0] = o + shape[1] * mybir.dt.size(dtype)
        return th

    WG = p2place("WG", [128, 8 * II], BF16)
    WU = p2place("WU", [128, 8 * II], BF16)
    AQF = [p2place(f"AQF{i}", [128, 8 * 512], BF16) for i in range(2)]
    hh = p2place("hh", [128, II], F32)
    tbA = [p2place(f"tbA{i}", [128, 512], F32) for i in range(2)]
    tbB = [p2place(f"tbB{i}", [128, 512], F32) for i in range(2)]
    aq2b = p2place("aq2b", [128, II], I8)
    stagF2 = [p2place(f"stagF2_{i}", [128, 512], F32) for i in range(2)]
    qi8b = p2place("qi8b", [128, 512], I8)
    assert o2[0] <= CAP, f"pass2a overflow {o2[0]} > {CAP}"

    o3 = [G1]

    def p3place(name, shape, dtype):
        o = (o3[0] + 63) // 64 * 64
        th = nc.alloc_sbuf_tensor_at(name, shape, dtype, offset=BASE + o)
        o3[0] = o + shape[1] * mybir.dt.size(dtype)
        return th

    WD = p3place("WD", [128, 32 * D], BF16)
    AQ2F = [p3place(f"AQ2F{i}", [128, 32 * 512], BF16) for i in range(2)]
    yb2 = [p3place(f"yb2_{i}", [128, D], F32) for i in range(2)]
    outb = [p3place(f"outb{i}", [128, D], F32) for i in range(2)]
    preb = p3place("preb", [128, D], F32)
    dmp3 = p3place("dmp3", [128, D], F32)
    assert o3[0] <= CAP, f"pass2b overflow {o3[0]} > {CAP}"

    REG = [nc.alloc_psum_tensor(f"REG{i}", [128, D], F32) for i in range(3)]
    PSA = nc.alloc_psum_tensor("PSA", [128, D], F32R)

    def smc(i):
        return sm[:, i:i + 1]

    C = dict(sumx=0, mu=1, negmu=2, ssq=3, var=4, lnv=5, rstd=6,
             m1=7, negm1=8, se1=9, r1=10, m2=11, negm2=12, se2=13, r2=14,
             am=15, amc=16, rec=17, pmean=18, chh=19, c2=20, t1=21, t2=22,
             recw=23,
             dv1=40, dv2=41, ds1=42, ds2=43, dg1=44, dg2=45)
    p.dummy["vector"] = lambda v: v.tensor_copy(out=smc(C["dv2"]), in_=smc(C["dv1"]))
    p.dummy["scalar"] = lambda s: s.copy(smc(C["ds2"]), smc(C["ds1"]))
    p.dummy["gpsimd"] = lambda g: g.tensor_copy(out=smc(C["dg2"]), in_=smc(C["dg1"]))

    def half(buf, h):
        return buf[:, h * 512:(h + 1) * 512]

    tk = {}
    tk["gcols"] = p.op("sync", lambda s: s.dma_start(out=gcols[:], in_=gcols_h),
                       inc="sq", inck=16)
    tk["bcols"] = p.op("sync", lambda s: s.dma_start(out=cst[:, 0:8], in_=bcols_h),
                       inc="sq", inck=16)
    tk["ident"] = p.op("sync", lambda s: s.dma_start(out=cst[:, 8:136], in_=ident_h),
                       inc="sq", inck=16)
    tk["utri"] = p.op("sync", lambda s: s.dma_start(out=cst[:, 136:264], in_=utri_h),
                      inc="sq", inck=16)
    tk["invs"] = p.op("sync", lambda s: s.dma_start(out=invs[:], in_=invs_h),
                      inc="sq", inck=16)
    for nm in ("bq", "bk", "bv", "bqa", "bka"):
        tk[nm] = p.op("gpsimd", lambda g, nm=nm: g.dma_start(out=vrow[nm][:],
                                                             in_=vr_h[nm]),
                      inc="gq", inck=16)

    p.op("vector", lambda v: v.memset(sm[:, 40:48], 0.0), writes=["smdum"], inc="dve")
    p.op("vector", lambda v: v.tensor_copy(out=ident_r[:], in_=cst[:, 8:136]),
         waits=[tk["ident"]], reads=["cst"], writes=["identr"], inc="dve")
    p.op("vector", lambda v: v.tensor_copy(out=utri_r[:], in_=cst[:, 136:264]),
         waits=[tk["utri"]], reads=["cst"], writes=["utrir"], inc="dve")
    p.op("vector", lambda v: v.memset(cst[:, 264:392], 1.0), writes=["cst1"], inc="dve")
    p.op("vector", lambda v: v.tensor_copy(out=allon_r[:], in_=cst[:, 264:392]),
         reads=["cst1"], writes=["allon"], inc="dve")
    p.op("vector", lambda v: v.memset(allon_f[:], 1.0), writes=["allonf"], inc="dve")
    p.op("vector", lambda v: v.tensor_copy(out=bcols_r[:], in_=cst[:, 0:8]),
         waits=[tk["bcols"]], reads=["cst8"], writes=["bcolsr"], inc="dve")
    p.op("vector", lambda v: v.tensor_copy(out=bcols_bf[:], in_=cst[:, 0:8]),
         reads=["cst8"], writes=["bcolsbf"], inc="dve")
    p.op("vector", lambda v: v.memset(stagB[:], 0.0), writes=["stB0"], inc="dve")
    cq_tok = p.op("vector", lambda v: v.tensor_copy(out=carryQ[:], in_=stagB[:]),
                  reads=["stB0"], writes=["carryQ"], inc="dve")
    ck_tok = p.op("vector", lambda v: v.tensor_copy(out=carryK[:], in_=stagB[:]),
                  reads=["stB0"], writes=["carryK"], inc="dve")
    p.op("vector", lambda v: v.memset(crow_bf[0:1, 0:128], 1.0), writes=["crbf"],
         inc="dve")
    ones_tok = p.op("vector", lambda v: v.tensor_copy(out=ones_row,
                                                      in_=crow_bf[0:1, 0:128]),
                    reads=["crbf"], writes=["onesr"], inc="dve")
    bqar_tok = p.op("vector", lambda v: v.tensor_scalar(
        bqa_row, vrow["bqa"][:], 1.0 / 32.0, None, ALU.mult),
        waits=[tk["bqa"]], reads=["vbqa"], writes=["bqar"], inc="dve")
    bkar_tok = p.op("vector", lambda v: v.tensor_scalar(
        bka_row, vrow["bka"][:], 1.0 / 32.0, None, ALU.mult),
        waits=[tk["bka"]], reads=["vbka"], writes=["bkar"], inc="dve")

    tko = {}
    reg2_free = None
    stg_free = [None, None]

    def prep_attn(nm, dst, f32r, bias_nm):
        nonlocal reg2_free
        bias_last = None
        last = None
        for ki in range(8):
            b = ki % 2
            stg = stagA if b == 0 else stagB
            ld = p.op("sync", lambda s, ki=ki, stg=stg, nm=nm: s.dma_start(
                out=stg[:], in_=wh[nm][ki * 128:(ki + 1) * 128, :]),
                waits=[stg_free[b]], inc="sq", inck=16)
            ct = p.op("vector", lambda v, ki=ki, stg=stg, dst=dst: v.tensor_copy(
                out=dst[:, ki * D:(ki + 1) * D], in_=stg[:]),
                waits=[ld], reads=[f"st{b}"], writes=[f"{nm}{ki}"], inc="dve")
            if bias_nm is not None:
                bc = bcols_r if f32r else bcols_bf
                for hh_ in range(2):
                    bias_last = p.op("tensor",
                                     lambda te, ki=ki, h=hh_, bc=bc, dst=dst: te.matmul(
                                         REG[2][0:1, h * 512:(h + 1) * 512],
                                         bc[:, ki:ki + 1],
                                         dst[:, ki * D + h * 512:ki * D + h * 512 + 512],
                                         start=(ki == 0), stop=(ki == 7)),
                                     waits=[ct, reg2_free,
                                            tk["bcols"]], inc="pe")
            if bias_nm is not None or nm in ("wq", "wk", "wv"):
                ft = p.op("vector", lambda v, ki=ki, dst=dst: v.tensor_scalar(
                    dst[:, ki * D:(ki + 1) * D], dst[:, ki * D:(ki + 1) * D],
                    gcols[:, ki:ki + 1], None, ALU.mult),
                    waits=[bias_last, tk["gcols"]],
                    reads=[f"{nm}{ki}"], writes=[f"{nm}{ki}"], inc="dve")
                last = ft
                stg_free[b] = ct
            else:
                last = ct
                stg_free[b] = ct
        if bias_nm is not None:
            rt = p.op("vector", lambda v: v.tensor_tensor(
                crow_bf[:], REG[2][0:1, 0:D], vrow[bias_nm][:], ALU.add),
                waits=[bias_last, tk[bias_nm]], reads=["psr"], writes=["crb2"],
                inc="dve")
            bl = None
            for hh_ in range(2):
                bl = p.op("tensor", lambda te, h=hh_: te.matmul(
                    REG[2][:, h * 512:(h + 1) * 512], ones_row,
                    crow_bf[0:1, h * 512:(h + 1) * 512], start=True, stop=True),
                    waits=[rt, ones_tok], inc="pe")
            et = p.op("scalar", lambda sc, bias_nm=bias_nm: sc.activation(
                btile[bias_nm][:], REG[2][:, 0:D], AF.Identity, bias=0.0, scale=1.0),
                waits=[bl], writes=["bt" + bias_nm], inc="act")
            reg2_free = et
            last = [last, et]
        return last

    tko["wq"] = prep_attn("wq", W["wq"], True, "bq")
    tko["wk"] = prep_attn("wk", W["wk"], True, "bk")
    tko["wv"] = prep_attn("wv", W["wv"], False, "bv")
    for nm in ("wqa", "wka"):
        last = None
        for ki in range(8):
            b = ki % 2
            stg = stagA if b == 0 else stagB
            ld = p.op("sync", lambda s, ki=ki, stg=stg, nm=nm: s.dma_start(
                out=stg[:], in_=wh[nm][ki * 128:(ki + 1) * 128, :]),
                waits=[stg_free[b]], inc="sq", inck=16)
            last = p.op("vector", lambda v, ki=ki, stg=stg, nm=nm: v.tensor_scalar(
                W[nm][:, ki * D:(ki + 1) * D], stg[:], 1.0 / 32.0, None, ALU.mult),
                waits=[ld], reads=[f"st{b}"], writes=[f"{nm}{ki}"], inc="dve")
            stg_free[b] = last
        tko[nm] = last

    p.barrier()

    WROWS = {"g": II, "u": II, "d": D}
    WCOLS = {"g": D, "u": D, "d": II}
    WIDX = {"g": 0, "u": 1, "d": 2}

    def slab_ap(hbm, k, s):
        ncolh = WCOLS[k] // 512
        r, h = s // ncolh, s % ncolh
        return hbm[r * 128:(r + 1) * 128, h * 512:(h + 1) * 512]

    class PrepCtx:
        pass

    def make_prep(k, stg2, qi, psa_hook, shared):
        pc = PrepCtx()
        nslab = (WROWS[k] * WCOLS[k]) // (128 * 512)
        widx = WIDX[k]

        def stats_one(s):
            b = s % 2
            ld = p.op("scalar", lambda sc, s=s, b=b: sc.dma_start(
                out=stg2[b][:], in_=slab_ap(wf_h[k], k, s)),
                waits=[shared["free"][b]], inc="lq", inck=16)
            st = p.op("scalar", lambda sc, s=s, b=b: sc.activation(
                stg2[b][:], stg2[b][:], AF.Abs, bias=0.0, scale=1.0,
                accum_out=pst[:, s:s + 1]),
                waits=[ld], reads=[f"pf{b}"], writes=[f"pf{b}", "pst"], inc="act")
            shared["free"][b] = st

        def final():
            t1 = p.op("vector", lambda v: v.reduce_sum(smc(C["pmean"]), pst[:, 0:64],
                                                       axis=AX.X),
                      reads=["pst"], writes=["pmean"], inc="dve")
            psa_tok = psa_hook()
            t3 = p.op("tensor", lambda te: te.matmul(
                PSA[:, 0:1].bitcast(F32), allon_f[:], smc(C["pmean"]),
                start=True, stop=True),
                waits=[t1, psa_tok], inc="pe")
            t4 = p.op("scalar", lambda sc: sc.activation(
                smc(C["pmean"]), PSA[:, 0:1].bitcast(F32), AF.Identity,
                bias=0.0, scale=1.0),
                waits=[t3], writes=["pmean"], inc="act")
            psa_hook(set_tok=t4)
            t5 = p.op("gpsimd", lambda g: g.tensor_scalar(
                wsm[:, widx:widx + 1], smc(C["pmean"]),
                1.0 / (WROWS[k] * WCOLS[k]), EPS, ALU.mult, ALU.max),
                waits=[t4], reads=["pmean"], writes=[f"wm{widx}"], inc="gp")
            t6 = p.op("vector", lambda v: v.reciprocal(wsm[:, 3 + widx:4 + widx],
                                                       wsm[:, widx:widx + 1]),
                      waits=[t5], reads=[f"wm{widx}"], writes=[f"rw{widx}"], inc="dve")
            pc.rec_tok = t6

        def quant_one(s):
            b = s % 2
            rcol = wsm[:, 3 + widx:4 + widx]
            ld = p.op("scalar", lambda sc, s=s, b=b: sc.dma_start(
                out=stg2[b][:], in_=slab_ap(wf_h[k], k, s)),
                waits=[shared["free"][b]], inc="lq", inck=16)
            g1 = p.op("vector", lambda v, b=b, rcol=rcol: v.tensor_scalar(
                stg2[b][:], stg2[b][:], rcol, -1.49, ALU.mult, ALU.max),
                waits=[ld, pc.rec_tok], reads=[f"pf{b}", f"rw{widx}"],
                writes=[f"pf{b}"], inc="dve")
            d1 = p.op("vector", lambda v, b=b: v.tensor_scalar(
                qi[:], stg2[b][:], 1.49, None, ALU.min),
                waits=[shared["qfree"]], reads=[f"pf{b}"], writes=["qi8"],
                inc="dve")
            shared["free"][b] = d1
            stt_ = p.op("gpsimd", lambda g, s=s: g.dma_start(
                out=slab_ap(wq_scr[k], k, s), in_=qi[:]),
                waits=[d1], inc="gq", inck=16)
            shared["qfree"] = stt_
            pc.last_store = stt_

        return pc, [lambda s=s: stats_one(s) for s in range(nslab)], final, \
            [lambda s=s: quant_one(s) for s in range(nslab)]

    tokst = {"PSA_free": None}

    def psa_hook(set_tok=None):
        if set_tok is not None:
            tokst["PSA_free"] = set_tok
            return None
        return tokst["PSA_free"]

    prep_shared1 = {"free": [None, None], "qfree": None}
    pcg, stats_g, final_g, quant_g = make_prep("g", stagF, qi8, psa_hook, prep_shared1)
    pcu, stats_u, final_u, quant_u = make_prep("u", stagF, qi8, psa_hook, prep_shared1)

    prep_plan = {i: [] for i in range(T + 2)}
    if not debug_pass1:
        def spread(fns, lo, hi):
            nt = hi - lo + 1
            for idx, fn in enumerate(fns):
                prep_plan[lo + (idx * nt) // len(fns)].append(fn)
        spread(stats_g, 1, 7)
        prep_plan[8].append(final_g)
        spread(quant_g, 9, 16)
        spread(stats_u, 9, 16)
        prep_plan[17].append(final_u)
        spread(quant_u, 18, 26)

    class Pool:
        def __init__(self, entries):
            self.q = list(entries)

        def get(self):
            return self.q.pop(0)

        def put(self, r, tok):
            self.q.append((r, tok))

    pool = Pool([(REG[0], None), (REG[1], None), (REG[2], reg2_free)])

    st = {
        "zdump_free": None, "qbuf_free": None,
        "kbuf_avail": None, "vbuf_avail": None, "abuf_free": None,
        "a_tok": None, "b_tok": None,
        "aq_store": None, "carryQ": cq_tok, "carryK": ck_tok,
        "shfm": None, "shbf": None, "qbuf_tok": None,
        "xfree": [None, None],
    }
    y_store_toks = [None] * T
    aq_store_toks = [None] * T

    for it in range(T + 1):
        i, jm = it, it - 1

        if i < T:
            tok_x = p.op("sync", lambda s, i=i: s.dma_start(
                out=(xbuf if i % 2 == 0 else xres)[:],
                in_=x_h[i * 128:(i + 1) * 128, :]),
                waits=[st["xfree"][i % 2]], inc="sq", inck=16)
            t1 = p.op("scalar", lambda sc, i=i: sc.activation(
                PSA[:].bitcast(F32), (xbuf if i % 2 == 0 else xres)[:], AF.Copy,
                accum_out=smc(C["sumx"])),
                waits=[tok_x, tokst["PSA_free"]],
                reads=["xbuf"], writes=["psaD", "sumx"], inc="act")
            p.op("gpsimd", lambda g: g.tensor_scalar(
                smc(C["mu"]), smc(C["sumx"]), 1.0 / D, None, ALU.mult),
                waits=[t1], reads=["sumx"], writes=["mu"], inc="gp")
            t2 = p.op("gpsimd", lambda g: g.tensor_scalar(
                smc(C["negmu"]), smc(C["sumx"]), -1.0 / D, None, ALU.mult),
                reads=["sumx"], writes=["negmu"], inc="gp")
            t3 = p.op("scalar", lambda sc, i=i: sc.activation(
                PSA[:].bitcast(F32), (xbuf if i % 2 == 0 else xres)[:], AF.Square,
                bias=smc(C["negmu"]), scale=1.0, accum_out=smc(C["ssq"])),
                waits=[t2], reads=["xbuf", "negmu"], writes=["psaD", "ssq"],
                inc="act")
            tokst["PSA_free"] = t3
            t4 = p.op("gpsimd", lambda g: g.tensor_scalar(
                smc(C["var"]), smc(C["ssq"]), 1.0 / D, EPS, ALU.mult, ALU.add),
                waits=[t3], reads=["ssq"], writes=["var"], inc="gp")
            t5 = p.op("scalar", lambda sc: sc.activation(
                smc(C["lnv"]), smc(C["var"]), AF.Ln, bias=0.0, scale=1.0),
                waits=[t4], reads=["var"], writes=["lnv"], inc="act")
            t6 = p.op("scalar", lambda sc: sc.activation(
                smc(C["rstd"]), smc(C["lnv"]), AF.Exp, bias=0.0, scale=-0.5),
                reads=["lnv"], writes=["rstd"], inc="act")
            t_z = p.op("vector", lambda v, i=i: v.tensor_scalar(
                zr[:], (xbuf if i % 2 == 0 else xres)[:], smc(C["mu"]), smc(C["rstd"]),
                ALU.subtract, ALU.mult),
                waits=[t6, st["zdump_free"]],
                reads=["xbuf", "mu", "rstd"], writes=["zdump"], inc="dve")
            tt = None
            for c in range(8):
                tt = p.op("tensor", lambda te, c=c: te.transpose(
                    PSA[:, c * 128:(c + 1) * 128], zr[:, c * 128:(c + 1) * 128],
                    ident_r[:]),
                    waits=[t_z, tokst["PSA_free"]], inc="pe")
            st["zdump_free"] = tt
            st["shfm"] = p.op("scalar", lambda sc: sc.activation(
                shfm[:], PSA[:], AF.Identity, bias=0.0, scale=1.0),
                waits=[tt], writes=["shfm"], inc="act")
            tokst["PSA_free"] = st["shfm"]
            st["shbf"] = p.op("vector", lambda v: v.tensor_copy(out=shbf[:], in_=shfm[:]),
                              waits=[st["shfm"], st.get("vmm_prev")],
                              reads=["shfm"], writes=["shbf"], inc="dve")

        if i < T:
            rq, rqf = pool.get()
            mm = None
            for ki in range(8):
                for h in range(2):
                    mm = p.op("tensor", lambda te, ki=ki, h=h, rq=rq: te.matmul(
                        half(rq, h), shfm[:, ki * 128:(ki + 1) * 128],
                        W["wq"][:, ki * D + h * 512:ki * D + h * 512 + 512],
                        start=(ki == 0), stop=(ki == 7)),
                        waits=[st["shfm"], rqf, tko["wq"]], inc="pe")
            st["qbuf_tok"] = p.op("vector", lambda v, rq=rq: v.scalar_tensor_tensor(
                qbuf[:], rq[:], 1.0, btile["bq"][:], ALU.mult, ALU.add),
                waits=[mm, st["qbuf_free"]], reads=["btq"], writes=["qbuf"],
                inc="dve")
            pool.put(rq, st["qbuf_tok"])
            rk, rkf = pool.get()
            for ki in range(8):
                for h in range(2):
                    mm = p.op("tensor", lambda te, ki=ki, h=h, rk=rk: te.matmul(
                        half(rk, h), shfm[:, ki * 128:(ki + 1) * 128],
                        W["wk"][:, ki * D + h * 512:ki * D + h * 512 + 512],
                        start=(ki == 0), stop=(ki == 7)),
                        waits=[rkf, tko["wk"]], inc="pe")
            st["kbuf_avail"] = p.op("vector", lambda v, rk=rk: v.scalar_tensor_tensor(
                kbuf[:], rk[:], 1.0, btile["bk"][:], ALU.mult, ALU.add),
                waits=[mm, st.get("kr_free")], reads=["btk"], writes=["kbuf"],
                inc="dve")
            pool.put(rk, st["kbuf_avail"])

        if jm >= 0:
            ra, raf = pool.get()
            for h in range(2):
                p.op("tensor", lambda te, h=h, ra=ra: te.matmul(
                    half(ra, h), ident_r[:], half(carryK, h),
                    start=True, stop=False),
                    waits=[st["b_tok"], st["carryK"], raf], inc="pe")
            mm = None
            for h in range(2):
                mm = p.op("tensor", lambda te, h=h, ra=ra: te.matmul(
                    half(ra, h), utri_r[:], half(abuf, h),
                    start=False, stop=True),
                    waits=[st["b_tok"]], inc="pe")
            cum2 = mm
            rb, rbf = pool.get()
            for h in range(2):
                mm = p.op("tensor", lambda te, h=h, rb=rb: te.matmul(
                    half(rb, h), allon_r[:], half(abuf, h),
                    start=True, stop=True),
                    waits=[rbf], inc="pe")
            col2 = mm
            st["abuf_free"] = col2
            t_ao = p.op("vector", lambda v, ra=ra, jm=jm: v.scalar_tensor_tensor(
                zdump[:], ra[:], invs[:, jm:jm + 1], vbuf[:], ALU.mult, ALU.mult),
                waits=[cum2, st["vbuf_avail"], st["zdump_free"]],
                reads=["vbuf"], writes=["zdump"], inc="dve")
            pool.put(ra, t_ao)
            st["carryK"] = p.op("vector", lambda v, rb=rb: v.tensor_tensor(
                carryK[:], carryK[:], rb[:], ALU.add),
                waits=[col2], reads=["carryK"], writes=["carryK"], inc="dve")
            pool.put(rb, st["carryK"])
            t_res = p.op("vector", lambda v, jm=jm: v.tensor_tensor(
                ybuf[:], zdump[:], (xbuf if jm % 2 == 0 else xres)[:], ALU.add),
                waits=[t_ao, col2],
                reads=["zdump", "xbuf"], writes=["abuf"], inc="dve")
            st["xfree"][jm % 2] = t_res
            t1 = p.op("scalar", lambda sc: sc.activation(
                PSA[:].bitcast(F32), ybuf[:], AF.Copy, accum_out=smc(C["sumx"])),
                waits=[t_res, tokst["PSA_free"]],
                reads=["abuf"], writes=["psaD", "sumx"], inc="act")
            p.op("gpsimd", lambda g: g.tensor_scalar(
                smc(C["mu"]), smc(C["sumx"]), 1.0 / D, None, ALU.mult),
                waits=[t1], reads=["sumx"], writes=["mu"], inc="gp")
            t2 = p.op("gpsimd", lambda g: g.tensor_scalar(
                smc(C["negmu"]), smc(C["sumx"]), -1.0 / D, None, ALU.mult),
                reads=["sumx"], writes=["negmu"], inc="gp")
            t3 = p.op("scalar", lambda sc: sc.activation(
                PSA[:].bitcast(F32), ybuf[:], AF.Square, bias=smc(C["negmu"]),
                scale=1.0, accum_out=smc(C["ssq"])),
                waits=[t2], reads=["abuf", "negmu"], writes=["psaD", "ssq"],
                inc="act")
            tokst["PSA_free"] = t3
            st["zdump_free"] = t_res
            t4 = p.op("gpsimd", lambda g: g.tensor_scalar(
                smc(C["var"]), smc(C["ssq"]), 1.0 / D, EPS, ALU.mult, ALU.add),
                waits=[t3], reads=["ssq"], writes=["var"], inc="gp")
            t5 = p.op("scalar", lambda sc: sc.activation(
                smc(C["lnv"]), smc(C["var"]), AF.Ln, bias=0.0, scale=1.0),
                waits=[t4], reads=["var"], writes=["lnv"], inc="act")
            t6 = p.op("scalar", lambda sc: sc.activation(
                smc(C["rstd"]), smc(C["lnv"]), AF.Exp, bias=0.0, scale=-0.5),
                reads=["lnv"], writes=["rstd"], inc="act")
            t_yn = p.op("vector", lambda v: v.tensor_scalar(
                ybuf[:], ybuf[:], smc(C["mu"]), smc(C["rstd"]),
                ALU.subtract, ALU.mult),
                waits=[t6], reads=["abuf", "mu", "rstd"], writes=["abuf"],
                inc="dve")
            yst = p.op("gpsimd", lambda g, jm=jm: g.dma_start(
                out=y_scr[jm * 128:(jm + 1) * 128, :], in_=ybuf[:]),
                waits=[t_yn], inc="gq", inck=16)
            y_store_toks[jm] = yst
            t_am = p.op("vector", lambda v: v.reduce_max(
                smc(C["am"]), ybuf[:], axis=AX.X, apply_absolute_value=True),
                waits=[t_yn], reads=["abuf"], writes=["am"], inc="dve")
            t_amc = p.op("gpsimd", lambda g: g.tensor_scalar(
                smc(C["amc"]), smc(C["am"]), EPS, None, ALU.max),
                waits=[t_am], reads=["am"], writes=["amc"], inc="gp")
            p.op("gpsimd", lambda g, jm=jm: g.tensor_scalar(
                inv_s1[:, jm:jm + 1], smc(C["amc"]), 1.0 / 127.0, None, ALU.mult),
                reads=["amc"], writes=["invs1"], inc="gp")
            t_rec = p.op("vector", lambda v: v.reciprocal(smc(C["rec"]), smc(C["amc"])),
                         waits=[t_amc], reads=["amc"], writes=["rec"], inc="dve")
            t_aq = p.op("vector", lambda v: v.tensor_scalar(
                aqbuf[:], ybuf[:], smc(C["rec"]), 127.0, ALU.mult, ALU.mult),
                waits=[t_rec, st["aq_store"]],
                reads=["abuf", "rec"], writes=["aqbuf"], inc="dve")
            st["aq_store"] = p.op("gpsimd", lambda g, jm=jm: g.dma_start(
                out=aq_scr[jm * 128:(jm + 1) * 128, :], in_=aqbuf[:]),
                waits=[t_aq], inc="gq", inck=16)
            aq_store_toks[jm] = st["aq_store"]
            st["y_gate"] = yst

        if i < T:
            tt = None
            for c in range(8):
                tt = p.op("tensor", lambda te, c=c: te.transpose(
                    PSA[:, c * 128:(c + 1) * 128], qbuf[:, c * 128:(c + 1) * 128],
                    ident_r[:]),
                    waits=[st["qbuf_tok"], tokst["PSA_free"]], inc="pe")
            st["shfm"] = p.op("scalar", lambda sc: sc.activation(
                shfm[:], PSA[:], AF.Identity, bias=0.0, scale=1.0),
                waits=[tt], writes=["shfm"], inc="act")
            tokst["PSA_free"] = st["shfm"]
            rqa, rqaf = pool.get()
            for h in range(2):
                p.op("tensor", lambda te, h=h, rqa=rqa: te.matmul(
                    half(rqa, h), ones_row, bqa_row[0:1, h * 512:(h + 1) * 512],
                    start=True, stop=False),
                    waits=[ones_tok, bqar_tok, rqaf], inc="pe")
            mm = None
            for ki in range(8):
                for h in range(2):
                    mm = p.op("tensor", lambda te, ki=ki, h=h, rqa=rqa: te.matmul(
                        half(rqa, h), shfm[:, ki * 128:(ki + 1) * 128],
                        W["wqa"][:, ki * D + h * 512:ki * D + h * 512 + 512],
                        start=False, stop=(ki == 7)),
                        waits=[st["shfm"], tko["wqa"]], inc="pe")
            t_nm1 = p.op("vector", lambda v, rqa=rqa: v.reduce_max(
                smc(C["negm1"]), rqa[:], axis=AX.X, negate=True),
                waits=[mm], writes=["negm1"], inc="dve")
            t_e1 = p.op("scalar", lambda sc, rqa=rqa: sc.activation(
                zdump[:], rqa[:], AF.Exp, bias=smc(C["negm1"]), scale=1.0,
                accum_out=smc(C["se1"])),
                waits=[t_nm1, st["zdump_free"]],
                reads=["negm1"], writes=["zdump", "se1"], inc="act")
            pool.put(rqa, t_e1)
            t_r1 = p.op("vector", lambda v: v.reciprocal(smc(C["r1"]), smc(C["se1"])),
                        waits=[t_e1], reads=["se1"], writes=["r1"], inc="dve")
            st["a_tok"] = p.op("vector", lambda v: v.scalar_tensor_tensor(
                abuf[:], zdump[:], smc(C["r1"]), qbuf[:], ALU.mult, ALU.mult),
                waits=[t_r1, st["abuf_free"], st.get("y_gate")],
                reads=["zdump", "r1", "qbuf"], writes=["abuf"], inc="dve")
            st["zdump_free"] = st["a_tok"]

        if i < T:
            rv, rvf = pool.get()
            mm = None
            for ki in range(8):
                for h in range(2):
                    mm = p.op("tensor", lambda te, ki=ki, h=h, rv=rv: te.matmul(
                        half(rv, h), shbf[:, ki * 128:(ki + 1) * 128],
                        W["wv"][:, ki * D + h * 512:ki * D + h * 512 + 512],
                        start=(ki == 0), stop=(ki == 7)),
                        waits=[st["shbf"], rvf, tko["wv"]], inc="pe")
            st["vmm_prev"] = mm
            st["vbuf_avail"] = p.op("vector", lambda v, rv=rv: v.scalar_tensor_tensor(
                vbuf[:], rv[:], 1.0, btile["bv"][:], ALU.mult, ALU.add),
                waits=[mm], reads=["btv"], writes=["vbuf"], inc="dve")
            pool.put(rv, st["vbuf_avail"])
            rc, rcf = pool.get()
            for h in range(2):
                p.op("tensor", lambda te, h=h, rc=rc: te.matmul(
                    half(rc, h), ident_r[:], half(carryQ, h),
                    start=True, stop=False),
                    waits=[st["a_tok"], st["carryQ"], rcf], inc="pe")
            mm = None
            for h in range(2):
                mm = p.op("tensor", lambda te, h=h, rc=rc: te.matmul(
                    half(rc, h), utri_r[:], half(abuf, h),
                    start=False, stop=True),
                    waits=[st["a_tok"]], inc="pe")
            cum1 = mm
            rd, rdf = pool.get()
            for h in range(2):
                mm = p.op("tensor", lambda te, h=h, rd=rd: te.matmul(
                    half(rd, h), allon_r[:], half(abuf, h),
                    start=True, stop=True),
                    waits=[rdf], inc="pe")
            col1 = mm
            st["abuf_free"] = col1
            t_mx = p.op("vector", lambda v, rc=rc, i=i: v.scalar_tensor_tensor(
                kr[:], rc[:], invs[:, i:i + 1], kbuf[:], ALU.mult, ALU.mult),
                waits=[cum1, st["kbuf_avail"]],
                reads=["kbuf"], writes=["kbuf"], inc="dve")
            pool.put(rc, t_mx)
            st["carryQ"] = p.op("vector", lambda v, rd=rd: v.tensor_tensor(
                carryQ[:], carryQ[:], rd[:], ALU.add),
                waits=[col1], reads=["carryQ"], writes=["carryQ"], inc="dve")
            pool.put(rd, st["carryQ"])
            tt = None
            for c in range(8):
                tt = p.op("tensor", lambda te, c=c: te.transpose(
                    PSA[:, c * 128:(c + 1) * 128], kr[:, c * 128:(c + 1) * 128],
                    ident_r[:]),
                    waits=[t_mx, tokst["PSA_free"]], inc="pe")
            tmix_pe = tt
            st["shfm"] = p.op("scalar", lambda sc: sc.activation(
                shfm[:], PSA[:], AF.Identity, bias=0.0, scale=1.0),
                waits=[tt], writes=["shfm"], inc="act")
            tokst["PSA_free"] = st["shfm"]
            rka, rkaf = pool.get()
            for h in range(2):
                p.op("tensor", lambda te, h=h, rka=rka: te.matmul(
                    half(rka, h), ones_row, bka_row[0:1, h * 512:(h + 1) * 512],
                    start=True, stop=False),
                    waits=[ones_tok, bkar_tok, rkaf], inc="pe")
            mm = None
            for ki in range(8):
                for h in range(2):
                    mm = p.op("tensor", lambda te, ki=ki, h=h, rka=rka: te.matmul(
                        half(rka, h), shfm[:, ki * 128:(ki + 1) * 128],
                        W["wka"][:, ki * D + h * 512:ki * D + h * 512 + 512],
                        start=False, stop=(ki == 7)),
                        waits=[st["shfm"], tko["wka"]], inc="pe")
            t_nm2 = p.op("vector", lambda v, rka=rka: v.reduce_max(
                smc(C["negm2"]), rka[:], axis=AX.X, negate=True),
                waits=[mm], writes=["negm2"], inc="dve")
            t_e2 = p.op("scalar", lambda sc, rka=rka: sc.activation(
                zdump[:], rka[:], AF.Exp, bias=smc(C["negm2"]), scale=1.0,
                accum_out=smc(C["se2"])),
                waits=[t_nm2, st["zdump_free"]],
                reads=["negm2"], writes=["zdump", "se2"], inc="act")
            pool.put(rka, t_e2)
            t_r2 = p.op("vector", lambda v: v.reciprocal(smc(C["r2"]), smc(C["se2"])),
                        waits=[t_e2], reads=["se2"], writes=["r2"], inc="dve")
            st["b_tok"] = p.op("vector", lambda v: v.scalar_tensor_tensor(
                abuf[:], zdump[:], smc(C["r2"]), kr[:], ALU.mult, ALU.mult),
                waits=[t_r2, st["abuf_free"]],
                reads=["zdump", "r2", "kbuf"], writes=["abuf"], inc="dve")
            st["zdump_free"] = st["b_tok"]
            st["qbuf_free"] = st["a_tok"]
            st["kr_free"] = [st["b_tok"], tmix_pe]

        for fn in prep_plan.get(it, []):
            fn()

    if debug_pass1:
        total_gq = p.semv["gq"]
        p.op("gpsimd", lambda g: g.tensor_copy(out=smc(C["dg2"]), in_=smc(C["dg1"])),
             waits=[("gq", total_gq)], inc="gp")
        return finish(nc, p)

    p.barrier()
    prep_shared2 = {"free": [None, None], "qfree": None}
    pcd, stats_d, final_d, quant_d = make_prep("d", stagF2, qi8b, psa_hook,
                                               prep_shared2)
    prep2_plan = {j: [] for j in range(T + 1)}

    def spread2(fns, lo, hi):
        nt = hi - lo + 1
        for idx, fn in enumerate(fns):
            prep2_plan[lo + (idx * nt) // len(fns)].append(fn)
    spread2(stats_d, 0, 7)
    prep2_plan[8].append(final_d)
    spread2(quant_d, 9, 20)

    p.op("gpsimd", lambda g: g.tensor_tensor(
        wsm[:, 6:7], wsm[:, 0:1], wsm[:, 0:1], ALU.mult),
        reads=["wm0"], writes=["wm6"], inc="gp")
    w2_tok = p.op("gpsimd", lambda g: g.tensor_tensor(
        wsm[:, 6:7], wsm[:, 6:7], wsm[:, 1:2], ALU.mult),
        reads=["wm6", "wm1"], writes=["wm6"], inc="gp")

    wg_tok = None
    for hc in range(8):
        wg_tok = p.op("sync", lambda s, hc=hc: s.dma_start(
            out=WG[:, hc * II:(hc + 1) * II],
            in_=wq_scr["g"][0:II, hc * 128:(hc + 1) * 128], transpose=True),
            inc="sq", inck=16)
        wg_tok = p.op("sync", lambda s, hc=hc: s.dma_start(
            out=WU[:, hc * II:(hc + 1) * II],
            in_=wq_scr["u"][0:II, hc * 128:(hc + 1) * 128], transpose=True),
            inc="sq", inck=16)

    aqf_tok = [None, None]
    aqf_free = [None, None]
    ps_free = [None, None]
    hh_frees = []
    aq2_store = None
    chh_free = None
    aq2_store_toks = [None] * T
    for cch in range(8):
        b = cch % 2
        mmld = None
        for hc in range(8):
            mmld = p.op("sync", lambda s, cch=cch, hc=hc, b=b: s.dma_start(
                out=AQF[b][:, hc * 512:(hc + 1) * 512],
                in_=aq_scr[cch * 512:(cch + 1) * 512, hc * 128:(hc + 1) * 128],
                transpose=True),
                waits=[aqf_free[b], wg_tok], inc="sq", inck=16)
        aqf_tok[b] = mmld
        last_mm_of_chunk = None
        for tl in range(4):
            j = cch * 4 + tl
            p.op("gpsimd", lambda g, j=j: g.tensor_tensor(
                smc(C["t1"]), inv_s1[:, j:j + 1], inv_s1[:, j:j + 1], ALU.mult),
                writes=["t1c"], inc="gp")
            p.op("gpsimd", lambda g, j=j: g.tensor_tensor(
                smc(C["t2"]), smc(C["t1"]), inv_s1[:, j:j + 1], ALU.mult),
                reads=["t1c"], writes=["t2c"], inc="gp")
            chh_tok = p.op("gpsimd", lambda g: g.tensor_tensor(
                smc(C["chh"]), smc(C["t2"]), wsm[:, 6:7], ALU.mult),
                waits=[w2_tok, chh_free], reads=["t2c", "wm6"], writes=["chh"],
                inc="gp")
            tile_dve = []
            for ig in range(8):
                pb = ig % 2
                psg = half(REG[pb], 0)
                psu = half(REG[pb], 1)
                mm = None
                for ki in range(8):
                    mm = p.op("tensor", lambda te, ki=ki, ig=ig, b=b, tl=tl,
                              psg=psg: te.matmul(
                        psg, AQF[b][:, ki * 512 + tl * 128:ki * 512 + tl * 128 + 128],
                        WG[:, ki * II + ig * 512:ki * II + ig * 512 + 512],
                        start=(ki == 0), stop=(ki == 7)),
                        waits=[aqf_tok[b], ps_free[pb]], inc="pe")
                for ki in range(8):
                    mm = p.op("tensor", lambda te, ki=ki, ig=ig, b=b, tl=tl,
                              psu=psu: te.matmul(
                        psu, AQF[b][:, ki * 512 + tl * 128:ki * 512 + tl * 128 + 128],
                        WU[:, ki * II + ig * 512:ki * II + ig * 512 + 512],
                        start=(ki == 0), stop=(ki == 7)),
                        inc="pe")
                last_mm_of_chunk = mm
                t_rl = p.op("scalar", lambda sc, pb=pb, psg=psg: sc.activation(
                    tbA[pb][:], psg, AF.Relu, bias=0.0, scale=1.0),
                    waits=[mm], writes=[f"tbA{pb}"], inc="act")
                t_sq = p.op("vector", lambda v, pb=pb: v.scalar_tensor_tensor(
                    tbB[pb][:], tbA[pb][:], smc(C["chh"]), tbA[pb][:],
                    ALU.mult, ALU.mult),
                    waits=[t_rl, chh_tok],
                    reads=[f"tbA{pb}", "chh"], writes=[f"tbB{pb}"], inc="dve")
                t_hh = p.op("vector", lambda v, pb=pb, ig=ig, psu=psu:
                            v.scalar_tensor_tensor(
                    hh[:, ig * 512:(ig + 1) * 512], psu, 1.0, tbB[pb][:],
                    ALU.mult, ALU.mult),
                    waits=[t_sq] + (hh_frees if ig == 0 else []),
                    reads=[f"tbB{pb}", "hhfree"], writes=[f"hh{ig}"], inc="dve")
                tile_dve.append(t_hh)
                ps_free[pb] = t_hh
            t_am = p.op("vector", lambda v: v.reduce_max(
                smc(C["am"]), hh[:], axis=AX.X, apply_absolute_value=True),
                waits=[tile_dve[-1]],
                reads=[f"hh{7}"], writes=["am"], inc="dve")
            t_amc = p.op("gpsimd", lambda g: g.tensor_scalar(
                smc(C["amc"]), smc(C["am"]), EPS, None, ALU.max),
                waits=[t_am], reads=["am"], writes=["amc"], inc="gp")
            p.op("gpsimd", lambda g, j=j: g.tensor_scalar(
                inv_s2[:, j:j + 1], smc(C["amc"]), 1.0 / 127.0, None, ALU.mult),
                reads=["amc"], writes=["invs2"], inc="gp")
            t_rec = p.op("vector", lambda v: v.reciprocal(smc(C["rec"]), smc(C["amc"])),
                         waits=[t_amc], reads=["amc"], writes=["rec"], inc="dve")
            t_q2 = p.op("vector", lambda v: v.tensor_scalar(
                aq2b[:], hh[:], smc(C["rec"]), 127.0, ALU.mult, ALU.mult),
                waits=[t_rec, aq2_store], reads=["rec"], writes=["aq2b"], inc="dve")
            aq2_store = p.op("gpsimd", lambda g, j=j: g.dma_start(
                out=aq2_scr[j * 128:(j + 1) * 128, :], in_=aq2b[:]),
                waits=[t_q2], inc="gq", inck=16)
            aq2_store_toks[j] = aq2_store
            hh_frees = [t_q2]
            chh_free = t_sq
            for fn in prep2_plan.get(j, []):
                fn()
        aqf_free[b] = last_mm_of_chunk

    p.barrier()
    wd_tok = None
    for ic in range(32):
        wd_tok = p.op("sync", lambda s, ic=ic: s.dma_start(
            out=WD[:, ic * D:(ic + 1) * D],
            in_=wq_scr["d"][0:D, ic * 128:(ic + 1) * 128], transpose=True),
            waits=[pcd.last_store], inc="sq", inck=16)

    a2f_tok = [None, None]
    a2f_free = [None, None]
    pd_free = [None, None]
    yb_free = [None, None]
    ob_free = [None, None]
    preb_free = None
    c2_free = None
    dmp3_free = None
    out_last = None
    for cch in range(8):
        b = cch % 2
        mmld = None
        for ic in range(32):
            mmld = p.op("sync", lambda s, cch=cch, ic=ic, b=b: s.dma_start(
                out=AQ2F[b][:, ic * 512:(ic + 1) * 512],
                in_=aq2_scr[cch * 512:(cch + 1) * 512, ic * 128:(ic + 1) * 128],
                transpose=True),
                waits=[a2f_free[b], wd_tok], inc="sq", inck=16)
        a2f_tok[b] = mmld
        last_mm_of_chunk = None
        for tl in range(4):
            j = cch * 4 + tl
            jb = j % 2
            t_y = p.op("scalar", lambda sc, j=j, jb=jb: sc.dma_start(
                out=yb2[jb][:], in_=y_scr[j * 128:(j + 1) * 128, :]),
                waits=[yb_free[jb]], inc="lq", inck=16)
            c2_tok = p.op("gpsimd", lambda g, j=j: g.tensor_tensor(
                smc(C["c2"]), inv_s2[:, j:j + 1], wsm[:, 2:3], ALU.mult),
                waits=[c2_free], reads=["wm2"], writes=["c2c"], inc="gp")
            pd = REG[jb]
            mm = None
            for ic in range(32):
                for h in range(2):
                    mm = p.op("tensor", lambda te, ic=ic, h=h, b=b, tl=tl,
                              pd=pd: te.matmul(
                        half(pd, h),
                        AQ2F[b][:, ic * 512 + tl * 128:ic * 512 + tl * 128 + 128],
                        WD[:, ic * D + h * 512:ic * D + h * 512 + 512],
                        start=(ic == 0), stop=(ic == 31)),
                        waits=[a2f_tok[b], pd_free[jb]], inc="pe")
            last_mm_of_chunk = mm
            t_pre = p.op("vector", lambda v, pd=pd, jb=jb: v.scalar_tensor_tensor(
                preb[:], pd[:], smc(C["c2"]), yb2[jb][:], ALU.mult, ALU.add),
                waits=[mm, t_y, c2_tok, preb_free],
                reads=["c2c", f"yb{jb}"], writes=["preb"], inc="dve")
            pd_free[jb] = t_pre
            yb_free[jb] = t_pre
            c2_free = t_pre
            t1 = p.op("scalar", lambda sc: sc.activation(
                dmp3[:], preb[:], AF.Copy, accum_out=smc(C["sumx"])),
                waits=[t_pre, dmp3_free], reads=["preb"], writes=["dmp3", "sumx"],
                inc="act")
            p.op("gpsimd", lambda g: g.tensor_scalar(
                smc(C["mu"]), smc(C["sumx"]), 1.0 / D, None, ALU.mult),
                waits=[t1], reads=["sumx"], writes=["mu"], inc="gp")
            t2 = p.op("gpsimd", lambda g: g.tensor_scalar(
                smc(C["negmu"]), smc(C["sumx"]), -1.0 / D, None, ALU.mult),
                reads=["sumx"], writes=["negmu"], inc="gp")
            t3 = p.op("scalar", lambda sc: sc.activation(
                dmp3[:], preb[:], AF.Square, bias=smc(C["negmu"]), scale=1.0,
                accum_out=smc(C["ssq"])),
                waits=[t2], reads=["preb", "negmu"], writes=["dmp3", "ssq"],
                inc="act")
            dmp3_free = t3
            t4 = p.op("gpsimd", lambda g: g.tensor_scalar(
                smc(C["var"]), smc(C["ssq"]), 1.0 / D, EPS, ALU.mult, ALU.add),
                waits=[t3], reads=["ssq"], writes=["var"], inc="gp")
            t5 = p.op("scalar", lambda sc: sc.activation(
                smc(C["lnv"]), smc(C["var"]), AF.Ln, bias=0.0, scale=1.0),
                waits=[t4], reads=["var"], writes=["lnv"], inc="act")
            t6 = p.op("scalar", lambda sc: sc.activation(
                smc(C["rstd"]), smc(C["lnv"]), AF.Exp, bias=0.0, scale=-0.5),
                reads=["lnv"], writes=["rstd"], inc="act")
            t_on = p.op("vector", lambda v, jb=jb: v.tensor_scalar(
                outb[jb][:], preb[:], smc(C["mu"]), smc(C["rstd"]),
                ALU.subtract, ALU.mult),
                waits=[t6, ob_free[jb]],
                reads=["preb", "mu", "rstd"], writes=[f"ob{jb}"], inc="dve")
            preb_free = t_on
            out_last = p.op("gpsimd", lambda g, j=j, jb=jb: g.dma_start(
                out=out_h[j * 128:(j + 1) * 128, :], in_=outb[jb][:]),
                waits=[t_on], inc="gq", inck=16)
            ob_free[jb] = out_last
        a2f_free[b] = last_mm_of_chunk

    total_gq = p.semv["gq"]
    p.op("gpsimd", lambda g: g.tensor_copy(out=smc(C["dg2"]), in_=smc(C["dg1"])),
         waits=[("gq", total_gq)], inc="gp")
    return finish(nc, p)


def finish(nc, p):
    from contextlib import ExitStack
    with ExitStack() as cx:
        semh = {}
        for s in p.semv:
            semh[s] = cx.enter_context(nc.semaphore(s))
        block = cx.enter_context(nc.Block())
        p.replay(block, semh)
    return nc


_NC_CACHE = {}


def _get_nc(debug_pass1=False):
    key = bool(debug_pass1)
    if key not in _NC_CACHE:
        _NC_CACHE[key] = build(debug_pass1)
    return _NC_CACHE[key]


def _host_consts():
    ident = np.eye(128, dtype=np.float32)
    utri = np.triu(np.ones((128, 128), dtype=np.float32))
    idx = (np.arange(T, dtype=np.float64)[None, :] * 128
           + np.arange(128, dtype=np.float64)[:, None] + 1.0)
    invs = (1.0 / idx).astype(np.float32)
    return ident, utri, invs


def kernel(**inputs):
    _ = inputs["L0_attn_attention_mask"]
    hs = np.ascontiguousarray(np.asarray(inputs["L0_attn_hidden_states"],
                                         dtype=np.float32))
    gamma = np.asarray(inputs["L0_attn_gamma"], np.float32).reshape(-1)
    beta = np.asarray(inputs["L0_attn_beta"], np.float32).reshape(-1)
    ident, utri, invs = _host_consts()
    common = {
        "wq": np.ascontiguousarray(np.asarray(inputs["L0_attn_Wq"], np.float32)),
        "wk": np.ascontiguousarray(np.asarray(inputs["L0_attn_Wk"], np.float32)),
        "wv": np.ascontiguousarray(np.asarray(inputs["L0_attn_Wv"], np.float32)),
        "wqa": np.ascontiguousarray(np.asarray(inputs["L0_attn_Wqa"], np.float32)),
        "wka": np.ascontiguousarray(np.asarray(inputs["L0_attn_Wka"], np.float32)),
        "v_bq": np.ascontiguousarray(np.asarray(inputs["L0_attn_bq"], np.float32).reshape(1, D)),
        "v_bk": np.ascontiguousarray(np.asarray(inputs["L0_attn_bk"], np.float32).reshape(1, D)),
        "v_bv": np.ascontiguousarray(np.asarray(inputs["L0_attn_bv"], np.float32).reshape(1, D)),
        "v_bqa": np.ascontiguousarray(np.asarray(inputs["L0_attn_bqa"], np.float32).reshape(1, D)),
        "v_bka": np.ascontiguousarray(np.asarray(inputs["L0_attn_bka"], np.float32).reshape(1, D)),
        "gcolsc": np.ascontiguousarray(gamma.reshape(8, 128).T),
        "bcolsc": np.ascontiguousarray(beta.reshape(8, 128).T),
        "wgate": np.ascontiguousarray(np.asarray(inputs["L0_ffn_w_gate"], np.float32)),
        "wup": np.ascontiguousarray(np.asarray(inputs["L0_ffn_w_up"], np.float32)),
        "wdown": np.ascontiguousarray(np.asarray(inputs["L0_ffn_w_down"], np.float32)),
        "identc": ident, "utric": utri, "invsc": invs,
    }
    nc = _get_nc(False)
    in_maps = [dict(common, x=np.ascontiguousarray(hs[c])) for c in range(NC)]
    res = run_bass_kernel_spmd(nc, in_maps, core_ids=list(range(NC)))
    out = np.stack([np.asarray(r["out"], np.float32) for r in res.results], axis=0)
    return out


# version 14
# speedup vs baseline: 1.1055x; 1.1055x over previous
import numpy as np
import ml_dtypes
import concourse.bass as bass
import concourse.mybir as mybir
from concourse.bass_utils import run_bass_kernel_spmd

F32 = mybir.dt.float32
F32R = mybir.dt.float32r
BF16 = mybir.dt.bfloat16
I8 = mybir.dt.int8
AF = mybir.ActivationFunctionType
ALU = mybir.AluOpType
AX = mybir.AxisListType

S, D, II = 4096, 1024, 4096
T = S // 128
NC = 8
EPS = 1e-5
ENGS = ("sync", "scalar", "vector", "tensor", "gpsimd")


class Prog:
    def __init__(self):
        self.ops = {e: [] for e in ENGS}
        self.semv = {s: 0 for s in ("sqa", "sqb", "lqa", "lqb", "lqc",
                                    "gy", "gaq", "gwq", "gv",
                                    "pe", "act", "dve", "gp")}
        self.waited = {e: {} for e in ENGS}
        self.lastw = {e: frozenset() for e in ENGS}
        self.dummy = {}
        self.n_sep = 0

    def wait(self, eng, tok):
        if tok is None:
            return
        nm, v = tok
        if v <= 0 or self.waited[eng].get(nm, -1) >= v:
            return
        self.ops[eng].append(("w", nm, v))
        self.waited[eng][nm] = v

    def op(self, eng, fn, reads=(), writes=(), waits=(), inc=None, inck=1,
           settle=False):
        for tk in waits:
            if isinstance(tk, list):
                for t2 in tk:
                    self.wait(eng, t2)
            else:
                self.wait(eng, tk)
        if frozenset(reads) & self.lastw[eng]:
            df = self.dummy.get(eng)
            if df is not None:
                self.ops[eng].append(("o", df, None, 0))
                self.ops[eng].append(("o", df, None, 0))
                self.lastw[eng] = frozenset()
                self.n_sep += 1
        tok = None
        if settle and inc is not None:
            df = self.dummy[eng]
            self.ops[eng].append(("o", fn, None, 0))
            self.semv[inc] += inck
            tok = (inc, self.semv[inc])
            self.ops[eng].append(("o", df, inc, inck))
            self.lastw[eng] = frozenset(writes)
            return tok
        if inc is not None:
            self.semv[inc] += inck
            tok = (inc, self.semv[inc])
        self.ops[eng].append(("o", fn, inc, inck))
        self.lastw[eng] = frozenset(writes)
        return tok

    def barrier(self):
        toks = [(nm, v) for nm, v in self.semv.items()]
        for eng in ENGS:
            for tk in toks:
                self.wait(eng, tk)

    def replay(self, block, semh):
        for en in ENGS:
            def make(en):
                def _(engine):
                    for it in self.ops[en]:
                        if it[0] == "w":
                            engine.wait_ge(semh[it[1]], it[2])
                        else:
                            ins = it[1](engine)
                            if it[2] is not None:
                                ins.then_inc(semh[it[2]], it[3])
                return _
            getattr(block, en)(make(en))


def build(debug_pass1=False, debug_scr=False):
    nc = bass.Bass("TRN2", target_bir_lowering=False)

    x_h = nc.dram_tensor("x", [S, D], F32, kind="ExternalInput").ap()
    wh = {nm: nc.dram_tensor(nm, [D, D], F32, kind="ExternalInput").ap()
          for nm in ("wq", "wk", "wv", "wqa", "wka")}
    vr_h = {nm: nc.dram_tensor("v_" + nm, [1, D], F32, kind="ExternalInput").ap()
            for nm in ("bq", "bk", "bv", "bqa", "bka")}
    gcols_h = nc.dram_tensor("gcolsc", [128, 8], F32, kind="ExternalInput").ap()
    bcols_h = nc.dram_tensor("bcolsc", [128, 8], F32, kind="ExternalInput").ap()
    wf_h = {"g": nc.dram_tensor("wgate", [II, D], F32, kind="ExternalInput").ap(),
            "u": nc.dram_tensor("wup", [II, D], F32, kind="ExternalInput").ap(),
            "d": nc.dram_tensor("wdown", [D, II], F32, kind="ExternalInput").ap()}
    ident_h = nc.dram_tensor("identc", [128, 128], F32, kind="ExternalInput").ap()
    utri_h = nc.dram_tensor("utric", [128, 128], F32, kind="ExternalInput").ap()
    invs_h = nc.dram_tensor("invsc", [128, T], F32, kind="ExternalInput").ap()

    out_h = nc.dram_tensor("out", [S, D], F32, kind="ExternalOutput").ap()
    ysk = "ExternalOutput" if (debug_pass1 or debug_scr) else "Internal"
    wsk = "ExternalOutput" if debug_scr else "Internal"
    y_scr = nc.dram_tensor("y_scr", [S, D], F32, kind=ysk).ap()
    aq_scr = nc.dram_tensor("aq_scr", [S, D], BF16, kind=ysk).ap()
    wq_scr = {"g": nc.dram_tensor("wgq", [II, D], BF16, kind=wsk).ap(),
              "u": nc.dram_tensor("wuq", [II, D], BF16, kind=wsk).ap(),
              "d": nc.dram_tensor("wdq", [D, II], BF16, kind=wsk).ap()}
    aq2_scr = nc.dram_tensor("aq2", [S, II], BF16, kind=wsk).ap()
    wqi_scr = {"g": nc.dram_tensor("wgqi", [II, D], I8, kind=wsk).ap(),
               "u": nc.dram_tensor("wuqi", [II, D], I8, kind=wsk).ap(),
               "d": nc.dram_tensor("wdqi", [D, II], I8, kind=wsk).ap()}
    nc._dbg_scr = debug_scr

    p = Prog()

    BASE = (nc.sbuf_base + 63) // 64 * 64
    CAP = nc.sbuf_top - BASE
    off = [0]

    def place(name, shape, dtype, at=None, align=64):
        if at is None:
            off[0] = (off[0] + align - 1) // align * align
            at = off[0]
            off[0] += shape[1] * mybir.dt.size(dtype) if len(shape) == 2 else shape[0]
        th = nc.alloc_sbuf_tensor_at(name, shape, dtype, offset=BASE + at)
        return th, at

    sm, _ = place("sm", [128, 48], F32)
    wsm, _ = place("wsm", [128, 8], F32)
    inv_s1, _ = place("inv_s1", [128, T], F32)
    inv_s2, _ = place("inv_s2", [128, T], F32)
    pst, _ = place("pst", [128, 64], F32)
    allon_f, _ = place("allon_f", [128, 128], F32)
    dmabuf, _ = place("dmabuf", [128, 16], F32)
    G1 = (off[0] + 63) // 64 * 64

    off[0] = G1
    W = {}
    for nm in ("wq", "wk", "wqa", "wka"):
        W[nm], _ = place("W_" + nm, [128, 8 * D], F32R)
    W["wv"], _ = place("W_wv", [128, 8 * D], BF16)
    btile = {}
    for nm in ("bq", "bk", "bv"):
        btile[nm], _ = place("bt_" + nm, [128, D], BF16)
    rows_bf, _ = place("rows_bf", [1, 2 * D + 128], BF16)
    bqa_row = rows_bf[0:1, 0:D]
    bka_row = rows_bf[0:1, D:2 * D]
    ones_row = rows_bf[0:1, 2 * D:2 * D + 128]
    carryQ, _ = place("carryQ", [128, D], F32R)
    carryK, _ = place("carryK", [128, D], F32R)
    ident_r, _ = place("ident_r", [128, 128], F32R)
    utri_r, _ = place("utri_r", [128, 128], F32R)
    allon_r, _ = place("allon_r", [128, 128], F32R)
    invs, _ = place("invs", [128, T], F32)
    gcols, _ = place("gcolsb", [128, 8], F32)
    bcols_r, _ = place("bcols_r", [128, 8], F32R)
    bcols_bf, _ = place("bcols_bf", [128, 8], BF16)
    PH = (off[0] + 63) // 64 * 64

    off[0] = PH
    xbuf, _ = place("xbuf", [128, D], F32)
    xres, _ = place("xres", [128, D], F32)
    zdump, zoff = place("zdump", [128, D], F32)
    zr, _ = place("zr", [128, D], F32R, at=zoff)
    shfm, _ = place("shfm", [128, D], F32R)
    shbf, _ = place("shbf", [128, D], BF16)
    qbuf, _ = place("qbuf", [128, D], F32R)
    kbuf, koff = place("kbuf", [128, D], F32)
    kr, _ = place("kr", [128, D], F32R, at=koff)
    vbuf, _ = place("vbuf", [128, D], F32)
    abuf, aoff = place("abuf", [128, D], F32R)
    ybuf, _ = place("ybuf", [128, D], F32, at=aoff)
    aqbuf, _ = place("aqbuf", [128, D], I8)
    stagF = []
    for i in range(3):
        t_, _ = place(f"stagF{i}", [128, 512], F32)
        stagF.append(t_)
    qi8l = []
    for i in range(2):
        t_, _ = place(f"qi8_{i}", [128, 512], I8)
        qi8l.append(t_)
    assert off[0] <= CAP, f"pass1 overflow {off[0]} > {CAP}"
    import os
    if os.environ.get("KPRINT"):
        print("pass1 end:", off[0], "spare:", CAP - off[0])

    po = [PH]

    def pplace(name, shape, dtype):
        o = (po[0] + 63) // 64 * 64
        th = nc.alloc_sbuf_tensor_at(name, shape, dtype, offset=BASE + o)
        po[0] = o + shape[1] * mybir.dt.size(dtype)
        return th

    vrow = {nm: pplace("vr_" + nm, [1, D], F32)
            for nm in ("bq", "bk", "bv", "bqa", "bka")}
    crow_bf = pplace("crow_bf", [1, D], BF16)
    stagA = pplace("stagA", [128, D], F32)
    stagB = pplace("stagB", [128, D], F32)
    cst = pplace("cst", [128, 400], F32)
    assert po[0] <= CAP

    o2 = [G1]

    def p2place(name, shape, dtype):
        o = (o2[0] + 63) // 64 * 64
        th = nc.alloc_sbuf_tensor_at(name, shape, dtype, offset=BASE + o)
        o2[0] = o + shape[1] * mybir.dt.size(dtype)
        return th

    WG = p2place("WG", [128, 8 * II], BF16)
    WU = p2place("WU", [128, 8 * II], BF16)
    AQF = [p2place(f"AQF{i}", [128, 8 * 512], BF16) for i in range(2)]
    hh = p2place("hh", [128, II], F32)
    tbA = [p2place(f"tbA{i}", [128, 512], F32) for i in range(2)]
    tbB = [p2place(f"tbB{i}", [128, 512], F32) for i in range(2)]
    aq2b = p2place("aq2b", [128, II], I8)
    stagF2 = [p2place(f"stagF2_{i}", [128, 512], F32) for i in range(3)]
    qi8bl = [p2place(f"qi8b_{i}", [128, 512], I8) for i in range(2)]
    assert o2[0] <= CAP, f"pass2a overflow {o2[0]} > {CAP}"

    o3 = [G1]

    def p3place(name, shape, dtype):
        o = (o3[0] + 63) // 64 * 64
        th = nc.alloc_sbuf_tensor_at(name, shape, dtype, offset=BASE + o)
        o3[0] = o + shape[1] * mybir.dt.size(dtype)
        return th

    WD = p3place("WD", [128, 32 * D], BF16)
    AQ2F = [p3place(f"AQ2F{i}", [128, 32 * 512], BF16) for i in range(2)]
    yb2 = [p3place(f"yb2_{i}", [128, D], F32) for i in range(2)]
    outb = [p3place(f"outb{i}", [128, D], F32) for i in range(2)]
    preb = p3place("preb", [128, D], F32)
    dmp3 = p3place("dmp3", [128, D], F32)
    assert o3[0] <= CAP, f"pass2b overflow {o3[0]} > {CAP}"

    REG = [nc.alloc_psum_tensor(f"REG{i}", [128, D], F32) for i in range(3)]
    PSA = nc.alloc_psum_tensor("PSA", [128, D], F32R)

    def smc(i):
        return sm[:, i:i + 1]

    C = dict(sumx=0, mu=1, negmu=2, ssq=3, var=4, lnv=5, rstd=6,
             m1=7, negm1=8, se1=9, r1=10, m2=11, negm2=12, se2=13, r2=14,
             am=15, amc=16, rec=17, pmean=18, chh=19, c2=20, t1=21, t2=22,
             recw=23,
             dv1=40, dv2=41, ds1=42, ds2=43, dg1=44, dg2=45)
    p.dummy["vector"] = lambda v: v.tensor_copy(out=smc(C["dv2"]), in_=smc(C["dv1"]))
    p.dummy["scalar"] = lambda s: s.copy(smc(C["ds2"]), smc(C["ds1"]))
    p.dummy["gpsimd"] = lambda g: g.tensor_copy(out=smc(C["dg2"]), in_=smc(C["dg1"]))

    def half(buf, h):
        return buf[:, h * 512:(h + 1) * 512]

    tk = {}
    p.op("sync", lambda s: s.dma_start(out=gcols[:], in_=gcols_h), inc="sqa", inck=16)
    p.op("sync", lambda s: s.dma_start(out=cst[:, 0:8], in_=bcols_h), inc="sqa", inck=16)
    p.op("sync", lambda s: s.dma_start(out=cst[:, 8:136], in_=ident_h), inc="sqa", inck=16)
    p.op("sync", lambda s: s.dma_start(out=cst[:, 136:264], in_=utri_h), inc="sqa", inck=16)
    tok_consts = p.op("sync", lambda s: s.dma_start(out=invs[:], in_=invs_h),
                      inc="sqa", inck=16)
    for nm2 in ("gcols", "bcols", "ident", "utri", "invs"):
        tk[nm2] = tok_consts
    tok_vr = None
    for nm in ("bq", "bk", "bv", "bqa", "bka"):
        tok_vr = p.op("gpsimd", lambda g, nm=nm: g.dma_start(out=vrow[nm][:],
                                                             in_=vr_h[nm]),
                      inc="gv", inck=16)
    for nm in ("bq", "bk", "bv", "bqa", "bka"):
        tk[nm] = tok_vr

    p.op("vector", lambda v: v.memset(sm[:, 40:48], 0.0), writes=["smdum"], inc="dve")
    p.op("vector", lambda v: v.tensor_copy(out=ident_r[:], in_=cst[:, 8:136]),
         waits=[tk["ident"]], reads=["cst"], writes=["identr"], inc="dve")
    p.op("vector", lambda v: v.tensor_copy(out=utri_r[:], in_=cst[:, 136:264]),
         waits=[tk["utri"]], reads=["cst"], writes=["utrir"], inc="dve")
    p.op("vector", lambda v: v.memset(cst[:, 264:392], 1.0), writes=["cst1"], inc="dve")
    p.op("vector", lambda v: v.tensor_copy(out=allon_r[:], in_=cst[:, 264:392]),
         reads=["cst1"], writes=["allon"], inc="dve")
    p.op("vector", lambda v: v.memset(allon_f[:], 1.0), writes=["allonf"], inc="dve")
    p.op("vector", lambda v: v.tensor_copy(out=bcols_r[:], in_=cst[:, 0:8]),
         waits=[tk["bcols"]], reads=["cst8"], writes=["bcolsr"], inc="dve")
    p.op("vector", lambda v: v.tensor_copy(out=bcols_bf[:], in_=cst[:, 0:8]),
         reads=["cst8"], writes=["bcolsbf"], inc="dve")
    p.op("vector", lambda v: v.memset(stagB[:], 0.0), writes=["stB0"], inc="dve")
    cq_tok = p.op("vector", lambda v: v.tensor_copy(out=carryQ[:], in_=stagB[:]),
                  reads=["stB0"], writes=["carryQ"], inc="dve")
    ck_tok = p.op("vector", lambda v: v.tensor_copy(out=carryK[:], in_=stagB[:]),
                  reads=["stB0"], writes=["carryK"], inc="dve")
    p.op("vector", lambda v: v.memset(crow_bf[0:1, 0:128], 1.0), writes=["crbf"],
         inc="dve")
    ones_tok = p.op("vector", lambda v: v.tensor_copy(out=ones_row,
                                                      in_=crow_bf[0:1, 0:128]),
                    reads=["crbf"], writes=["onesr"], inc="dve")
    bqar_tok = p.op("vector", lambda v: v.tensor_scalar(
        bqa_row, vrow["bqa"][:], 1.0 / 32.0, None, ALU.mult),
        waits=[tk["bqa"]], reads=["vbqa"], writes=["bqar"], inc="dve")
    bkar_tok = p.op("vector", lambda v: v.tensor_scalar(
        bka_row, vrow["bka"][:], 1.0 / 32.0, None, ALU.mult),
        waits=[tk["bka"]], reads=["vbka"], writes=["bkar"], inc="dve")

    tko = {}
    reg2_free = None
    stg_free = [None, None]

    def prep_attn(nm, dst, f32r, bias_nm):
        nonlocal reg2_free
        bias_last = None
        last = None
        for ki in range(8):
            b = ki % 2
            stg = stagA if b == 0 else stagB
            ld = p.op("sync", lambda s, ki=ki, stg=stg, nm=nm: s.dma_start(
                out=stg[:], in_=wh[nm][ki * 128:(ki + 1) * 128, :]),
                waits=[stg_free[b]], inc=("sqa" if b == 0 else "sqb"), inck=16)
            ct = p.op("vector", lambda v, ki=ki, stg=stg, dst=dst: v.tensor_copy(
                out=dst[:, ki * D:(ki + 1) * D], in_=stg[:]),
                waits=[ld], reads=[f"st{b}"], writes=[f"{nm}{ki}"], inc="dve")
            if bias_nm is not None:
                bc = bcols_r if f32r else bcols_bf
                for hh_ in range(2):
                    bias_last = p.op("tensor",
                                     lambda te, ki=ki, h=hh_, bc=bc, dst=dst: te.matmul(
                                         REG[2][0:1, h * 512:(h + 1) * 512],
                                         bc[:, ki:ki + 1],
                                         dst[:, ki * D + h * 512:ki * D + h * 512 + 512],
                                         start=(ki == 0), stop=(ki == 7)),
                                     waits=[ct, reg2_free,
                                            tk["bcols"]], inc="pe")
            if bias_nm is not None or nm in ("wq", "wk", "wv"):
                ft = p.op("vector", lambda v, ki=ki, dst=dst: v.tensor_scalar(
                    dst[:, ki * D:(ki + 1) * D], dst[:, ki * D:(ki + 1) * D],
                    gcols[:, ki:ki + 1], None, ALU.mult),
                    waits=[bias_last, tk["gcols"]],
                    reads=[f"{nm}{ki}"], writes=[f"{nm}{ki}"], inc="dve")
                last = ft
                stg_free[b] = ct
            else:
                last = ct
                stg_free[b] = ct
        if bias_nm is not None:
            rt = p.op("vector", lambda v: v.tensor_tensor(
                crow_bf[:], REG[2][0:1, 0:D], vrow[bias_nm][:], ALU.add),
                waits=[bias_last, tk[bias_nm]], reads=["psr"], writes=["crb2"],
                inc="dve")
            bl = None
            for hh_ in range(2):
                bl = p.op("tensor", lambda te, h=hh_: te.matmul(
                    REG[2][:, h * 512:(h + 1) * 512], ones_row,
                    crow_bf[0:1, h * 512:(h + 1) * 512], start=True, stop=True),
                    waits=[rt, ones_tok], inc="pe")
            et = p.op("scalar", lambda sc, bias_nm=bias_nm: sc.activation(
                btile[bias_nm][:], REG[2][:, 0:D], AF.Identity, bias=0.0, scale=1.0),
                waits=[bl], writes=["bt" + bias_nm], inc="act")
            reg2_free = et
            last = [last, et]
        return last

    tko["wq"] = prep_attn("wq", W["wq"], True, "bq")
    tko["wk"] = prep_attn("wk", W["wk"], True, "bk")
    tko["wv"] = prep_attn("wv", W["wv"], False, "bv")
    for nm in ("wqa", "wka"):
        last = None
        for ki in range(8):
            b = ki % 2
            stg = stagA if b == 0 else stagB
            ld = p.op("sync", lambda s, ki=ki, stg=stg, nm=nm: s.dma_start(
                out=stg[:], in_=wh[nm][ki * 128:(ki + 1) * 128, :]),
                waits=[stg_free[b]], inc=("sqa" if b == 0 else "sqb"), inck=16)
            last = p.op("vector", lambda v, ki=ki, stg=stg, nm=nm: v.tensor_scalar(
                W[nm][:, ki * D:(ki + 1) * D], stg[:], 1.0 / 32.0, None, ALU.mult),
                waits=[ld], reads=[f"st{b}"], writes=[f"{nm}{ki}"], inc="dve")
            stg_free[b] = last
        tko[nm] = last

    p.barrier()

    WROWS = {"g": II, "u": II, "d": D}
    WCOLS = {"g": D, "u": D, "d": II}
    WIDX = {"g": 0, "u": 1, "d": 2}

    def slab_ap(hbm, k, s):
        ncolh = WCOLS[k] // 512
        r, h = s // ncolh, s % ncolh
        return hbm[r * 128:(r + 1) * 128, h * 512:(h + 1) * 512]

    class PrepCtx:
        pass

    def make_prep(k, stg2, qi, psa_hook, shared):
        pc = PrepCtx()
        nslab = (WROWS[k] * WCOLS[k]) // (128 * 512)
        widx = WIDX[k]

        def q_flush():
            return

        def stage(s, proc):
            b = shared.get("bctr", 0) % len(stg2)
            shared["bctr"] = shared.get("bctr", 0) + 1
            ld = p.op("scalar", lambda sc, s=s, b=b: sc.dma_start(
                out=stg2[b][:], in_=slab_ap(wf_h[k], k, s)),
                waits=[shared["free"][b]], inc=("lqa", "lqb", "lqc")[b], inck=16)
            pendp = shared.get("pend_proc")
            shared["pend_proc"] = None
            if pendp is not None:
                pendp(ld)
            shared["pend_proc"] = lambda margin: proc(s, b, margin)

        def drain():
            if shared.get("pend_proc") is None:
                return
            db_ = shared.get("bctr", 0) % 3
            shared["bctr"] = shared.get("bctr", 0) + 1
            dtok = p.op("scalar", lambda sc: sc.dma_start(
                out=dmabuf[:], in_=ident_h[0:128, 0:16]),
                waits=[shared["free"][db_]],
                inc=("lqa", "lqb", "lqc")[db_], inck=16)
            shared["free"][db_] = dtok
            pendp = shared.get("pend_proc")
            shared["pend_proc"] = None
            if pendp is not None:
                pendp(dtok)

        def stats_proc(s, b, margin):
            st = p.op("scalar", lambda sc, s=s, b=b: sc.activation(
                stg2[b][:], stg2[b][:], AF.Abs, bias=0.0, scale=1.0,
                accum_out=pst[:, s:s + 1]),
                waits=[margin] + ([shared.get("pst_free")] if s == 0 else []),
                reads=[f"pf{b}"], writes=[f"pf{b}", "pst"], inc="act")
            shared["free"][b] = st
            pc.stats_last = st

        def quant_proc(s, b, margin):
            rcol = wsm[:, 3 + widx:4 + widx]
            p.op("vector", lambda v, b=b, rcol=rcol: v.tensor_scalar(
                stg2[b][:], stg2[b][:], rcol, -1.49, ALU.mult, ALU.max),
                waits=[margin, pc.rec_tok], reads=[f"pf{b}", f"rw{widx}"],
                writes=[f"pf{b}"], inc="dve")
            qb = shared.get("qctr", 0) % 2
            shared["qctr"] = shared.get("qctr", 0) + 1
            qbuf_ = qi[qb]
            d1 = p.op("vector", lambda v, b=b, qbuf_=qbuf_: v.tensor_scalar(
                qbuf_[:], stg2[b][:], 1.49, None, ALU.min),
                waits=[shared["qfree"][qb]], reads=[f"pf{b}"], writes=[f"qi8{qb}"],
                inc="dve", settle=True)
            shared["free"][b] = d1
            stt_ = p.op("gpsimd", lambda g, s=s, qbuf_=qbuf_: g.dma_start(
                out=slab_ap(wq_scr[k], k, s), in_=qbuf_[:]),
                waits=[d1], inc="gwq", inck=16)
            if nc._dbg_scr:
                stt_ = p.op("gpsimd", lambda g, s=s, qbuf_=qbuf_: g.dma_start(
                    out=slab_ap(wqi_scr[k], k, s), in_=qbuf_[:]),
                    inc="gwq", inck=16)
            shared["qfree"][qb] = stt_
            pc.last_store = stt_

        def stats_one(s):
            stage(s, stats_proc)

        def quant_one(s):
            stage(s, quant_proc)

        def final():
            drain()
            t1 = p.op("vector", lambda v: v.reduce_sum(smc(C["pmean"]), pst[:, 0:64],
                                                       axis=AX.X),
                      waits=[pc.stats_last], reads=["pst"], writes=["pmean"],
                      inc="dve")
            shared["pst_free"] = t1
            psa_tok = psa_hook()
            t3 = p.op("tensor", lambda te: te.matmul(
                PSA[:, 0:1].bitcast(F32), allon_f[:], smc(C["pmean"]),
                start=True, stop=True),
                waits=[t1, psa_tok], inc="pe")
            t4 = p.op("scalar", lambda sc: sc.activation(
                smc(C["pmean"]), PSA[:, 0:1].bitcast(F32), AF.Identity,
                bias=0.0, scale=1.0),
                waits=[t3], writes=["pmean"], inc="act")
            psa_hook(set_tok=t4)
            t5 = p.op("gpsimd", lambda g: g.tensor_scalar(
                wsm[:, widx:widx + 1], smc(C["pmean"]),
                1.0 / (WROWS[k] * WCOLS[k]), EPS, ALU.mult, ALU.max),
                waits=[t4], reads=["pmean"], writes=[f"wm{widx}"], inc="gp",
                settle=True)
            t6 = p.op("vector", lambda v: v.reciprocal(wsm[:, 3 + widx:4 + widx],
                                                       wsm[:, widx:widx + 1]),
                      waits=[t5], reads=[f"wm{widx}"], writes=[f"rw{widx}"],
                      inc="dve")
            pc.rec_tok = t6

        pc.q_flush = q_flush
        shared["flushfn"] = drain
        return pc, [lambda s=s: stats_one(s) for s in range(nslab)] + [drain], \
            final, [lambda s=s: quant_one(s) for s in range(nslab)] + [drain]

    tokst = {"PSA_free": None}

    def psa_hook(set_tok=None):
        if set_tok is not None:
            tokst["PSA_free"] = set_tok
            return None
        return tokst["PSA_free"]

    prep_shared1 = {"free": [None, None, None], "qfree": [None, None]}
    pcg, stats_g, final_g, quant_g = make_prep("g", stagF, qi8l, psa_hook, prep_shared1)
    pcu, stats_u, final_u, quant_u = make_prep("u", stagF, qi8l, psa_hook, prep_shared1)

    prep_plan = {i: [] for i in range(T + 2)}
    if not debug_pass1:
        def spread(fns, lo, hi):
            nt = hi - lo + 1
            for idx, fn in enumerate(fns):
                prep_plan[lo + (idx * nt) // len(fns)].append(fn)
        spread(stats_g, 1, 7)
        prep_plan[8].append(final_g)
        spread(quant_g, 9, 16)
        spread(stats_u, 9, 16)
        prep_plan[17].append(final_u)
        spread(quant_u, 18, 26)

    class Pool:
        def __init__(self, entries):
            self.q = list(entries)

        def get(self):
            return self.q.pop(0)

        def put(self, r, tok):
            self.q.append((r, tok))

    pool = Pool([(REG[0], None), (REG[1], None), (REG[2], reg2_free)])

    st = {
        "zdump_free": None, "qbuf_free": None,
        "kbuf_avail": None, "vbuf_avail": None, "abuf_free": None,
        "a_tok": None, "b_tok": None,
        "aq_store": None, "carryQ": cq_tok, "carryK": ck_tok,
        "shfm": None, "shbf": None, "qbuf_tok": None,
        "xfree": [None, None], "xtok": [None, None],
    }
    y_store_toks = [None] * T
    aq_store_toks = [None] * T

    for it in range(T + 1):
        i, jm = it, it - 1
        _pl = prep_plan.get(it, [])
        _p3 = max(1, (len(_pl) + 2) // 3)
        _pl1, _pl2, _pl3 = _pl[:_p3], _pl[_p3:2 * _p3], _pl[2 * _p3:]

        if i < T:
            if i == 0:
                t_x0 = p.op("sync", lambda s: s.dma_start(
                    out=xbuf[:], in_=x_h[0:128, :]), inc="sqa", inck=16)
                t_x1 = p.op("sync", lambda s: s.dma_start(
                    out=xres[:], in_=x_h[128:256, :]), inc="sqb", inck=16)
                st["xtok"][0] = [t_x0, t_x1]
                st["xtok"][1] = [t_x0, t_x1]
            tok_x = st["xtok"][i % 2]
            t1 = p.op("scalar", lambda sc, i=i: sc.activation(
                PSA[:].bitcast(F32), (xbuf if i % 2 == 0 else xres)[:], AF.Copy,
                accum_out=smc(C["sumx"])),
                waits=[tok_x, tokst["PSA_free"]],
                reads=["xbuf"], writes=["psaD", "sumx"], inc="act")
            p.op("gpsimd", lambda g: g.tensor_scalar(
                smc(C["mu"]), smc(C["sumx"]), 1.0 / D, None, ALU.mult),
                waits=[t1], reads=["sumx"], writes=["mu"], inc="gp", settle=True)
            t2 = p.op("gpsimd", lambda g: g.tensor_scalar(
                smc(C["negmu"]), smc(C["sumx"]), -1.0 / D, None, ALU.mult),
                reads=["sumx"], writes=["negmu"], inc="gp", settle=True)
            t3 = p.op("scalar", lambda sc, i=i: sc.activation(
                PSA[:].bitcast(F32), (xbuf if i % 2 == 0 else xres)[:], AF.Square,
                bias=smc(C["negmu"]), scale=1.0, accum_out=smc(C["ssq"])),
                waits=[t2], reads=["xbuf", "negmu"], writes=["psaD", "ssq"],
                inc="act")
            tokst["PSA_free"] = t3
            t4 = p.op("gpsimd", lambda g: g.tensor_scalar(
                smc(C["var"]), smc(C["ssq"]), 1.0 / D, EPS, ALU.mult, ALU.add),
                waits=[t3], reads=["ssq"], writes=["var"], inc="gp", settle=True)
            t5 = p.op("scalar", lambda sc: sc.activation(
                smc(C["lnv"]), smc(C["var"]), AF.Ln, bias=0.0, scale=1.0),
                waits=[t4], reads=["var"], writes=["lnv"], inc="act")
            t6 = p.op("scalar", lambda sc: sc.activation(
                smc(C["rstd"]), smc(C["lnv"]), AF.Exp, bias=0.0, scale=-0.5),
                reads=["lnv"], writes=["rstd"], inc="act", settle=True)
            t_z = p.op("vector", lambda v, i=i: v.tensor_scalar(
                zr[:], (xbuf if i % 2 == 0 else xres)[:], smc(C["mu"]), smc(C["rstd"]),
                ALU.subtract, ALU.mult),
                waits=[t6, st["zdump_free"]],
                reads=["xbuf", "mu", "rstd"], writes=["zdump"], inc="dve", settle=True)
            tt = None
            for c in range(8):
                tt = p.op("tensor", lambda te, c=c: te.transpose(
                    PSA[:, c * 128:(c + 1) * 128], zr[:, c * 128:(c + 1) * 128],
                    ident_r[:]),
                    waits=[t_z, tokst["PSA_free"]], inc="pe")
            st["zdump_free"] = tt
            st["shfm"] = p.op("scalar", lambda sc: sc.activation(
                shfm[:], PSA[:], AF.Identity, bias=0.0, scale=1.0),
                waits=[tt], writes=["shfm"], inc="act")
            tokst["PSA_free"] = st["shfm"]
            st["shbf"] = p.op("vector", lambda v: v.tensor_copy(out=shbf[:], in_=shfm[:]),
                              waits=[st["shfm"], st.get("vmm_prev")],
                              reads=["shfm"], writes=["shbf"], inc="dve", settle=True)

        for fn in _pl1:
            fn()

        if i < T:
            rq, rqf = pool.get()
            mm = None
            for ki in range(8):
                for h in range(2):
                    mm = p.op("tensor", lambda te, ki=ki, h=h, rq=rq: te.matmul(
                        half(rq, h), shfm[:, ki * 128:(ki + 1) * 128],
                        W["wq"][:, ki * D + h * 512:ki * D + h * 512 + 512],
                        start=(ki == 0), stop=(ki == 7)),
                        waits=[st["shfm"], rqf, tko["wq"]], inc="pe")
            st["qbuf_tok"] = p.op("vector", lambda v, rq=rq: v.scalar_tensor_tensor(
                qbuf[:], rq[:], 1.0, btile["bq"][:], ALU.mult, ALU.add),
                waits=[mm, st["qbuf_free"]], reads=["btq"], writes=["qbuf"],
                inc="dve", settle=True)
            pool.put(rq, st["qbuf_tok"])
            rk, rkf = pool.get()
            for ki in range(8):
                for h in range(2):
                    mm = p.op("tensor", lambda te, ki=ki, h=h, rk=rk: te.matmul(
                        half(rk, h), shfm[:, ki * 128:(ki + 1) * 128],
                        W["wk"][:, ki * D + h * 512:ki * D + h * 512 + 512],
                        start=(ki == 0), stop=(ki == 7)),
                        waits=[rkf, tko["wk"]], inc="pe")
            st["kbuf_avail"] = p.op("vector", lambda v, rk=rk: v.scalar_tensor_tensor(
                kbuf[:], rk[:], 1.0, btile["bk"][:], ALU.mult, ALU.add),
                waits=[mm, st.get("kr_free")], reads=["btk"], writes=["kbuf"],
                inc="dve")
            pool.put(rk, st["kbuf_avail"])

        if jm >= 0:
            ra, raf = pool.get()
            for h in range(2):
                p.op("tensor", lambda te, h=h, ra=ra: te.matmul(
                    half(ra, h), ident_r[:], half(carryK, h),
                    start=True, stop=False),
                    waits=[st["b_tok"], st["carryK"], raf], inc="pe")
            mm = None
            for h in range(2):
                mm = p.op("tensor", lambda te, h=h, ra=ra: te.matmul(
                    half(ra, h), utri_r[:], half(abuf, h),
                    start=False, stop=True),
                    waits=[st["b_tok"]], inc="pe")
            cum2 = mm
            rb, rbf = pool.get()
            for h in range(2):
                mm = p.op("tensor", lambda te, h=h, rb=rb: te.matmul(
                    half(rb, h), allon_r[:], half(abuf, h),
                    start=True, stop=True),
                    waits=[rbf], inc="pe")
            col2 = mm
            st["abuf_free"] = col2
            t_ao = p.op("vector", lambda v, ra=ra, jm=jm: v.scalar_tensor_tensor(
                zdump[:], ra[:], invs[:, jm:jm + 1], vbuf[:], ALU.mult, ALU.mult),
                waits=[cum2, st["vbuf_avail"], st["zdump_free"]],
                reads=["vbuf"], writes=["zdump"], inc="dve")
            pool.put(ra, t_ao)
            st["carryK"] = p.op("vector", lambda v, rb=rb: v.tensor_tensor(
                carryK[:], carryK[:], rb[:], ALU.add),
                waits=[col2], reads=["carryK"], writes=["carryK"], inc="dve", settle=True)
            pool.put(rb, st["carryK"])
            t_res = p.op("vector", lambda v, jm=jm: v.tensor_tensor(
                ybuf[:], zdump[:], (xbuf if jm % 2 == 0 else xres)[:], ALU.add),
                waits=[t_ao, col2],
                reads=["zdump", "xbuf"], writes=["abuf"], inc="dve")
            st["xfree"][jm % 2] = t_res
            nxt = jm + 2
            if nxt < T:
                st["xtok"][nxt % 2] = p.op("sync", lambda s, nxt=nxt: s.dma_start(
                    out=(xbuf if nxt % 2 == 0 else xres)[:],
                    in_=x_h[nxt * 128:(nxt + 1) * 128, :]),
                    waits=[t_res], inc=("sqa" if nxt % 2 == 0 else "sqb"), inck=16)
            t1 = p.op("scalar", lambda sc: sc.activation(
                PSA[:].bitcast(F32), ybuf[:], AF.Copy, accum_out=smc(C["sumx"])),
                waits=[t_res, tokst["PSA_free"]],
                reads=["abuf"], writes=["psaD", "sumx"], inc="act")
            p.op("gpsimd", lambda g: g.tensor_scalar(
                smc(C["mu"]), smc(C["sumx"]), 1.0 / D, None, ALU.mult),
                waits=[t1], reads=["sumx"], writes=["mu"], inc="gp", settle=True)
            t2 = p.op("gpsimd", lambda g: g.tensor_scalar(
                smc(C["negmu"]), smc(C["sumx"]), -1.0 / D, None, ALU.mult),
                reads=["sumx"], writes=["negmu"], inc="gp", settle=True)
            t3 = p.op("scalar", lambda sc: sc.activation(
                PSA[:].bitcast(F32), ybuf[:], AF.Square, bias=smc(C["negmu"]),
                scale=1.0, accum_out=smc(C["ssq"])),
                waits=[t2], reads=["abuf", "negmu"], writes=["psaD", "ssq"],
                inc="act")
            tokst["PSA_free"] = t3
            st["zdump_free"] = t_res
            t4 = p.op("gpsimd", lambda g: g.tensor_scalar(
                smc(C["var"]), smc(C["ssq"]), 1.0 / D, EPS, ALU.mult, ALU.add),
                waits=[t3], reads=["ssq"], writes=["var"], inc="gp", settle=True)
            t5 = p.op("scalar", lambda sc: sc.activation(
                smc(C["lnv"]), smc(C["var"]), AF.Ln, bias=0.0, scale=1.0),
                waits=[t4], reads=["var"], writes=["lnv"], inc="act")
            t6 = p.op("scalar", lambda sc: sc.activation(
                smc(C["rstd"]), smc(C["lnv"]), AF.Exp, bias=0.0, scale=-0.5),
                reads=["lnv"], writes=["rstd"], inc="act", settle=True)
            t_yn = p.op("vector", lambda v: v.tensor_scalar(
                ybuf[:], ybuf[:], smc(C["mu"]), smc(C["rstd"]),
                ALU.subtract, ALU.mult),
                waits=[t6], reads=["abuf", "mu", "rstd"], writes=["abuf"],
                inc="dve", settle=True)
            yst = p.op("gpsimd", lambda g, jm=jm: g.dma_start(
                out=y_scr[jm * 128:(jm + 1) * 128, :], in_=ybuf[:]),
                waits=[t_yn], inc="gy", inck=16)
            y_store_toks[jm] = yst
            t_am = p.op("vector", lambda v: v.reduce_max(
                smc(C["am"]), ybuf[:], axis=AX.X, apply_absolute_value=True),
                waits=[t_yn], reads=["abuf"], writes=["am"], inc="dve")
            t_amc = p.op("gpsimd", lambda g: g.tensor_scalar(
                smc(C["amc"]), smc(C["am"]), EPS, None, ALU.max),
                waits=[t_am], reads=["am"], writes=["amc"], inc="gp", settle=True)
            p.op("gpsimd", lambda g, jm=jm: g.tensor_scalar(
                inv_s1[:, jm:jm + 1], smc(C["amc"]), 1.0 / 127.0, None, ALU.mult),
                reads=["amc"], writes=["invs1"], inc="gp")
            t_rec = p.op("vector", lambda v: v.reciprocal(smc(C["rec"]), smc(C["amc"])),
                         waits=[t_amc], reads=["amc"], writes=["rec"], inc="dve")
            t_aq = p.op("vector", lambda v: v.tensor_scalar(
                aqbuf[:], ybuf[:], smc(C["rec"]), 127.0, ALU.mult, ALU.mult),
                waits=[t_rec, st["aq_store"]],
                reads=["abuf", "rec"], writes=["aqbuf"], inc="dve", settle=True)
            st["aq_store"] = p.op("gpsimd", lambda g, jm=jm: g.dma_start(
                out=aq_scr[jm * 128:(jm + 1) * 128, :], in_=aqbuf[:]),
                waits=[t_aq], inc="gaq", inck=16)
            aq_store_toks[jm] = st["aq_store"]
            st["y_gate"] = yst

        if i < T:
            tt = None
            for c in range(8):
                tt = p.op("tensor", lambda te, c=c: te.transpose(
                    PSA[:, c * 128:(c + 1) * 128], qbuf[:, c * 128:(c + 1) * 128],
                    ident_r[:]),
                    waits=[st["qbuf_tok"], tokst["PSA_free"]], inc="pe")
            st["shfm"] = p.op("scalar", lambda sc: sc.activation(
                shfm[:], PSA[:], AF.Identity, bias=0.0, scale=1.0),
                waits=[tt], writes=["shfm"], inc="act")
            tokst["PSA_free"] = st["shfm"]
            rqa, rqaf = pool.get()
            for h in range(2):
                p.op("tensor", lambda te, h=h, rqa=rqa: te.matmul(
                    half(rqa, h), ones_row, bqa_row[0:1, h * 512:(h + 1) * 512],
                    start=True, stop=False),
                    waits=[ones_tok, bqar_tok, rqaf], inc="pe")
            mm = None
            for ki in range(8):
                for h in range(2):
                    mm = p.op("tensor", lambda te, ki=ki, h=h, rqa=rqa: te.matmul(
                        half(rqa, h), shfm[:, ki * 128:(ki + 1) * 128],
                        W["wqa"][:, ki * D + h * 512:ki * D + h * 512 + 512],
                        start=False, stop=(ki == 7)),
                        waits=[st["shfm"], tko["wqa"]], inc="pe")
            t_nm1 = p.op("vector", lambda v, rqa=rqa: v.reduce_max(
                smc(C["negm1"]), rqa[:], axis=AX.X, negate=True),
                waits=[mm], writes=["negm1"], inc="dve", settle=True)
            t_e1 = p.op("scalar", lambda sc, rqa=rqa: sc.activation(
                zdump[:], rqa[:], AF.Exp, bias=smc(C["negm1"]), scale=1.0,
                accum_out=smc(C["se1"])),
                waits=[t_nm1, st["zdump_free"]],
                reads=["negm1"], writes=["zdump", "se1"], inc="act")
            pool.put(rqa, t_e1)
            t_r1 = p.op("vector", lambda v: v.reciprocal(smc(C["r1"]), smc(C["se1"])),
                        waits=[t_e1], reads=["se1"], writes=["r1"], inc="dve")
            st["a_tok"] = p.op("vector", lambda v: v.scalar_tensor_tensor(
                abuf[:], zdump[:], smc(C["r1"]), qbuf[:], ALU.mult, ALU.mult),
                waits=[t_r1, st["abuf_free"], st.get("y_gate")],
                reads=["zdump", "r1", "qbuf"], writes=["abuf"], inc="dve", settle=True)
            st["zdump_free"] = st["a_tok"]

        for fn in _pl2:
            fn()

        if i < T:
            rv, rvf = pool.get()
            mm = None
            for ki in range(8):
                for h in range(2):
                    mm = p.op("tensor", lambda te, ki=ki, h=h, rv=rv: te.matmul(
                        half(rv, h), shbf[:, ki * 128:(ki + 1) * 128],
                        W["wv"][:, ki * D + h * 512:ki * D + h * 512 + 512],
                        start=(ki == 0), stop=(ki == 7)),
                        waits=[st["shbf"], rvf, tko["wv"]], inc="pe")
            st["vmm_prev"] = mm
            st["vbuf_avail"] = p.op("vector", lambda v, rv=rv: v.scalar_tensor_tensor(
                vbuf[:], rv[:], 1.0, btile["bv"][:], ALU.mult, ALU.add),
                waits=[mm], reads=["btv"], writes=["vbuf"], inc="dve")
            pool.put(rv, st["vbuf_avail"])
            rc, rcf = pool.get()
            for h in range(2):
                p.op("tensor", lambda te, h=h, rc=rc: te.matmul(
                    half(rc, h), ident_r[:], half(carryQ, h),
                    start=True, stop=False),
                    waits=[st["a_tok"], st["carryQ"], rcf], inc="pe")
            mm = None
            for h in range(2):
                mm = p.op("tensor", lambda te, h=h, rc=rc: te.matmul(
                    half(rc, h), utri_r[:], half(abuf, h),
                    start=False, stop=True),
                    waits=[st["a_tok"]], inc="pe")
            cum1 = mm
            rd, rdf = pool.get()
            for h in range(2):
                mm = p.op("tensor", lambda te, h=h, rd=rd: te.matmul(
                    half(rd, h), allon_r[:], half(abuf, h),
                    start=True, stop=True),
                    waits=[rdf], inc="pe")
            col1 = mm
            st["abuf_free"] = col1
            t_mx = p.op("vector", lambda v, rc=rc, i=i: v.scalar_tensor_tensor(
                kr[:], rc[:], invs[:, i:i + 1], kbuf[:], ALU.mult, ALU.mult),
                waits=[cum1, st["kbuf_avail"]],
                reads=["kbuf"], writes=["kbuf"], inc="dve", settle=True)
            pool.put(rc, t_mx)
            st["carryQ"] = p.op("vector", lambda v, rd=rd: v.tensor_tensor(
                carryQ[:], carryQ[:], rd[:], ALU.add),
                waits=[col1], reads=["carryQ"], writes=["carryQ"], inc="dve", settle=True)
            pool.put(rd, st["carryQ"])
            tt = None
            for c in range(8):
                tt = p.op("tensor", lambda te, c=c: te.transpose(
                    PSA[:, c * 128:(c + 1) * 128], kr[:, c * 128:(c + 1) * 128],
                    ident_r[:]),
                    waits=[t_mx, tokst["PSA_free"]], inc="pe")
            tmix_pe = tt
            st["shfm"] = p.op("scalar", lambda sc: sc.activation(
                shfm[:], PSA[:], AF.Identity, bias=0.0, scale=1.0),
                waits=[tt], writes=["shfm"], inc="act")
            tokst["PSA_free"] = st["shfm"]
            rka, rkaf = pool.get()
            for h in range(2):
                p.op("tensor", lambda te, h=h, rka=rka: te.matmul(
                    half(rka, h), ones_row, bka_row[0:1, h * 512:(h + 1) * 512],
                    start=True, stop=False),
                    waits=[ones_tok, bkar_tok, rkaf], inc="pe")
            mm = None
            for ki in range(8):
                for h in range(2):
                    mm = p.op("tensor", lambda te, ki=ki, h=h, rka=rka: te.matmul(
                        half(rka, h), shfm[:, ki * 128:(ki + 1) * 128],
                        W["wka"][:, ki * D + h * 512:ki * D + h * 512 + 512],
                        start=False, stop=(ki == 7)),
                        waits=[st["shfm"], tko["wka"]], inc="pe")
            t_nm2 = p.op("vector", lambda v, rka=rka: v.reduce_max(
                smc(C["negm2"]), rka[:], axis=AX.X, negate=True),
                waits=[mm], writes=["negm2"], inc="dve", settle=True)
            t_e2 = p.op("scalar", lambda sc, rka=rka: sc.activation(
                zdump[:], rka[:], AF.Exp, bias=smc(C["negm2"]), scale=1.0,
                accum_out=smc(C["se2"])),
                waits=[t_nm2, st["zdump_free"]],
                reads=["negm2"], writes=["zdump", "se2"], inc="act")
            pool.put(rka, t_e2)
            t_r2 = p.op("vector", lambda v: v.reciprocal(smc(C["r2"]), smc(C["se2"])),
                        waits=[t_e2], reads=["se2"], writes=["r2"], inc="dve")
            st["b_tok"] = p.op("vector", lambda v: v.scalar_tensor_tensor(
                abuf[:], zdump[:], smc(C["r2"]), kr[:], ALU.mult, ALU.mult),
                waits=[t_r2, st["abuf_free"]],
                reads=["zdump", "r2", "kbuf"], writes=["abuf"], inc="dve", settle=True)
            st["zdump_free"] = st["b_tok"]
            st["qbuf_free"] = st["a_tok"]
            st["kr_free"] = [st["b_tok"], tmix_pe]

        for fn in _pl3:
            fn()

    if debug_pass1:
        fin = [(nm, p.semv[nm]) for nm in ("gy", "gaq", "gwq", "gv")]
        p.op("gpsimd", lambda g: g.tensor_copy(out=smc(C["dg2"]), in_=smc(C["dg1"])),
             waits=fin, inc="gp")
        return finish(nc, p)

    p.barrier()
    prep_shared2 = {"free": [None, None, None], "qfree": [None, None]}
    pcd, stats_d, final_d, quant_d = make_prep("d", stagF2, qi8bl, psa_hook,
                                               prep_shared2)
    prep2_plan = {j: [] for j in range(T + 1)}

    def spread2(fns, lo, hi):
        nt = hi - lo + 1
        for idx, fn in enumerate(fns):
            prep2_plan[lo + (idx * nt) // len(fns)].append(fn)
    spread2(stats_d, 0, 7)
    prep2_plan[8].append(final_d)
    spread2(quant_d, 9, 20)

    p.op("gpsimd", lambda g: g.tensor_tensor(
        wsm[:, 6:7], wsm[:, 0:1], wsm[:, 0:1], ALU.mult),
        reads=["wm0"], writes=["wm6"], inc="gp")
    w2_tok = p.op("gpsimd", lambda g: g.tensor_tensor(
        wsm[:, 6:7], wsm[:, 6:7], wsm[:, 1:2], ALU.mult),
        reads=["wm6", "wm1"], writes=["wm6"], inc="gp")

    wg_tok = None
    for hc in range(8):
        wg_tok = p.op("sync", lambda s, hc=hc: s.dma_start(
            out=WG[:, hc * II:(hc + 1) * II],
            in_=wq_scr["g"][0:II, hc * 128:(hc + 1) * 128], transpose=True),
            inc="sqa", inck=16)
        wg_tok = p.op("sync", lambda s, hc=hc: s.dma_start(
            out=WU[:, hc * II:(hc + 1) * II],
            in_=wq_scr["u"][0:II, hc * 128:(hc + 1) * 128], transpose=True),
            inc="sqa", inck=16)

    wg_tok = p.op("sync", lambda s: s.dma_start(
        out=dmabuf[:], in_=ident_h[0:128, 0:16]), inc="sqa", inck=16)
    aqf_tok = [None, None]
    aqf_free = [None, None]
    ps_free = [None, None]
    hh_frees = []
    aq2_store = None
    chh_free = None
    aq2_store_toks = [None] * T
    for cch in range(8):
        b = cch % 2
        mmld = None
        for hc in range(8):
            mmld = p.op("sync", lambda s, cch=cch, hc=hc, b=b: s.dma_start(
                out=AQF[b][:, hc * 512:(hc + 1) * 512],
                in_=aq_scr[cch * 512:(cch + 1) * 512, hc * 128:(hc + 1) * 128],
                transpose=True),
                waits=[aqf_free[b], wg_tok],
                inc=("sqa" if b == 0 else "sqb"), inck=16)
        aqf_tok[b] = p.op("sync", lambda s, b=b: s.dma_start(
            out=dmabuf[:], in_=ident_h[0:128, 0:16]),
            inc=("sqa" if b == 0 else "sqb"), inck=16)
        last_mm_of_chunk = None
        for tl in range(4):
            j = cch * 4 + tl
            p.op("gpsimd", lambda g, j=j: g.tensor_tensor(
                smc(C["t1"]), inv_s1[:, j:j + 1], inv_s1[:, j:j + 1], ALU.mult),
                writes=["t1c"], inc="gp")
            p.op("gpsimd", lambda g, j=j: g.tensor_tensor(
                smc(C["t2"]), smc(C["t1"]), inv_s1[:, j:j + 1], ALU.mult),
                reads=["t1c"], writes=["t2c"], inc="gp")
            chh_tok = p.op("gpsimd", lambda g: g.tensor_tensor(
                smc(C["chh"]), smc(C["t2"]), wsm[:, 6:7], ALU.mult),
                waits=[w2_tok, chh_free], reads=["t2c", "wm6"], writes=["chh"],
                inc="gp", settle=True)
            tile_dve = []
            for ig in range(8):
                pb = ig % 2
                psg = half(REG[pb], 0)
                psu = half(REG[pb], 1)
                mm = None
                for ki in range(8):
                    mm = p.op("tensor", lambda te, ki=ki, ig=ig, b=b, tl=tl,
                              psg=psg: te.matmul(
                        psg, AQF[b][:, ki * 512 + tl * 128:ki * 512 + tl * 128 + 128],
                        WG[:, ki * II + ig * 512:ki * II + ig * 512 + 512],
                        start=(ki == 0), stop=(ki == 7)),
                        waits=[aqf_tok[b], ps_free[pb]], inc="pe")
                for ki in range(8):
                    mm = p.op("tensor", lambda te, ki=ki, ig=ig, b=b, tl=tl,
                              psu=psu: te.matmul(
                        psu, AQF[b][:, ki * 512 + tl * 128:ki * 512 + tl * 128 + 128],
                        WU[:, ki * II + ig * 512:ki * II + ig * 512 + 512],
                        start=(ki == 0), stop=(ki == 7)),
                        inc="pe")
                last_mm_of_chunk = mm
                t_rl = p.op("scalar", lambda sc, pb=pb, psg=psg: sc.activation(
                    tbA[pb][:], psg, AF.Relu, bias=0.0, scale=1.0),
                    waits=[mm], writes=[f"tbA{pb}"], inc="act")
                t_sq = p.op("vector", lambda v, pb=pb: v.scalar_tensor_tensor(
                    tbB[pb][:], tbA[pb][:], smc(C["chh"]), tbA[pb][:],
                    ALU.mult, ALU.mult),
                    waits=[t_rl, chh_tok],
                    reads=[f"tbA{pb}", "chh"], writes=[f"tbB{pb}"], inc="dve")
                t_hh = p.op("vector", lambda v, pb=pb, ig=ig, psu=psu:
                            v.scalar_tensor_tensor(
                    hh[:, ig * 512:(ig + 1) * 512], psu, 1.0, tbB[pb][:],
                    ALU.mult, ALU.mult),
                    waits=[t_sq] + (hh_frees if ig == 0 else []),
                    reads=[f"tbB{pb}", "hhfree"], writes=[f"hh{ig}"], inc="dve")
                tile_dve.append(t_hh)
                ps_free[pb] = t_hh
            t_am = p.op("vector", lambda v: v.reduce_max(
                smc(C["am"]), hh[:], axis=AX.X, apply_absolute_value=True),
                waits=[tile_dve[-1]],
                reads=[f"hh{7}"], writes=["am"], inc="dve")
            t_amc = p.op("gpsimd", lambda g: g.tensor_scalar(
                smc(C["amc"]), smc(C["am"]), EPS, None, ALU.max),
                waits=[t_am], reads=["am"], writes=["amc"], inc="gp", settle=True)
            p.op("gpsimd", lambda g, j=j: g.tensor_scalar(
                inv_s2[:, j:j + 1], smc(C["amc"]), 1.0 / 127.0, None, ALU.mult),
                reads=["amc"], writes=["invs2"], inc="gp")
            t_rec = p.op("vector", lambda v: v.reciprocal(smc(C["rec"]), smc(C["amc"])),
                         waits=[t_amc], reads=["amc"], writes=["rec"], inc="dve")
            t_q2 = p.op("vector", lambda v: v.tensor_scalar(
                aq2b[:], hh[:], smc(C["rec"]), 127.0, ALU.mult, ALU.mult),
                waits=[t_rec, aq2_store], reads=["rec"], writes=["aq2b"], inc="dve", settle=True)
            aq2_store = p.op("gpsimd", lambda g, j=j: g.dma_start(
                out=aq2_scr[j * 128:(j + 1) * 128, :], in_=aq2b[:]),
                waits=[t_q2], inc="gaq", inck=16)
            aq2_store_toks[j] = aq2_store
            hh_frees = [t_q2]
            chh_free = t_sq
            for fn in prep2_plan.get(j, []):
                fn()
        aqf_free[b] = last_mm_of_chunk

    p.barrier()
    wd_tok = None
    for ic in range(32):
        wd_tok = p.op("sync", lambda s, ic=ic: s.dma_start(
            out=WD[:, ic * D:(ic + 1) * D],
            in_=wq_scr["d"][0:D, ic * 128:(ic + 1) * 128], transpose=True),
            waits=[pcd.last_store], inc="sqa", inck=16)

    wd_tok = p.op("sync", lambda s: s.dma_start(
        out=dmabuf[:], in_=ident_h[0:128, 0:16]), inc="sqa", inck=16)
    a2f_tok = [None, None]
    a2f_free = [None, None]
    pd_free = [None, None]
    yb_free = [None, None]
    ob_free = [None, None]
    preb_free = None
    c2_free = None
    dmp3_free = None
    out_last = None
    for cch in range(8):
        b = cch % 2
        mmld = None
        for ic in range(32):
            mmld = p.op("sync", lambda s, cch=cch, ic=ic, b=b: s.dma_start(
                out=AQ2F[b][:, ic * 512:(ic + 1) * 512],
                in_=aq2_scr[cch * 512:(cch + 1) * 512, ic * 128:(ic + 1) * 128],
                transpose=True),
                waits=[a2f_free[b], wd_tok],
                inc=("sqa" if b == 0 else "sqb"), inck=16)
        a2f_tok[b] = p.op("sync", lambda s, b=b: s.dma_start(
            out=dmabuf[:], in_=ident_h[0:128, 0:16]),
            inc=("sqa" if b == 0 else "sqb"), inck=16)
        last_mm_of_chunk = None
        for tl in range(4):
            j = cch * 4 + tl
            jb = j % 2
            t_y = p.op("gpsimd", lambda g, j=j, jb=jb: g.dma_start(
                out=yb2[jb][:], in_=y_scr[j * 128:(j + 1) * 128, :]),
                waits=[yb_free[jb]], inc="gy", inck=16)
            c2_tok = p.op("gpsimd", lambda g, j=j: g.tensor_tensor(
                smc(C["c2"]), inv_s2[:, j:j + 1], wsm[:, 2:3], ALU.mult),
                waits=[c2_free], reads=["wm2"], writes=["c2c"], inc="gp", settle=True)
            pd = REG[jb]
            mm = None
            for ic in range(32):
                for h in range(2):
                    mm = p.op("tensor", lambda te, ic=ic, h=h, b=b, tl=tl,
                              pd=pd: te.matmul(
                        half(pd, h),
                        AQ2F[b][:, ic * 512 + tl * 128:ic * 512 + tl * 128 + 128],
                        WD[:, ic * D + h * 512:ic * D + h * 512 + 512],
                        start=(ic == 0), stop=(ic == 31)),
                        waits=[a2f_tok[b], pd_free[jb]], inc="pe")
            last_mm_of_chunk = mm
            t_pre = p.op("vector", lambda v, pd=pd, jb=jb: v.scalar_tensor_tensor(
                preb[:], pd[:], smc(C["c2"]), yb2[jb][:], ALU.mult, ALU.add),
                waits=[mm, t_y, c2_tok, preb_free],
                reads=["c2c", f"yb{jb}"], writes=["preb"], inc="dve")
            pd_free[jb] = t_pre
            yb_free[jb] = t_pre
            c2_free = t_pre
            t1 = p.op("scalar", lambda sc: sc.activation(
                dmp3[:], preb[:], AF.Copy, accum_out=smc(C["sumx"])),
                waits=[t_pre, dmp3_free], reads=["preb"], writes=["dmp3", "sumx"],
                inc="act")
            p.op("gpsimd", lambda g: g.tensor_scalar(
                smc(C["mu"]), smc(C["sumx"]), 1.0 / D, None, ALU.mult),
                waits=[t1], reads=["sumx"], writes=["mu"], inc="gp", settle=True)
            t2 = p.op("gpsimd", lambda g: g.tensor_scalar(
                smc(C["negmu"]), smc(C["sumx"]), -1.0 / D, None, ALU.mult),
                reads=["sumx"], writes=["negmu"], inc="gp", settle=True)
            t3 = p.op("scalar", lambda sc: sc.activation(
                dmp3[:], preb[:], AF.Square, bias=smc(C["negmu"]), scale=1.0,
                accum_out=smc(C["ssq"])),
                waits=[t2], reads=["preb", "negmu"], writes=["dmp3", "ssq"],
                inc="act")
            dmp3_free = t3
            t4 = p.op("gpsimd", lambda g: g.tensor_scalar(
                smc(C["var"]), smc(C["ssq"]), 1.0 / D, EPS, ALU.mult, ALU.add),
                waits=[t3], reads=["ssq"], writes=["var"], inc="gp", settle=True)
            t5 = p.op("scalar", lambda sc: sc.activation(
                smc(C["lnv"]), smc(C["var"]), AF.Ln, bias=0.0, scale=1.0),
                waits=[t4], reads=["var"], writes=["lnv"], inc="act")
            t6 = p.op("scalar", lambda sc: sc.activation(
                smc(C["rstd"]), smc(C["lnv"]), AF.Exp, bias=0.0, scale=-0.5),
                reads=["lnv"], writes=["rstd"], inc="act", settle=True)
            t_on = p.op("vector", lambda v, jb=jb: v.tensor_scalar(
                outb[jb][:], preb[:], smc(C["mu"]), smc(C["rstd"]),
                ALU.subtract, ALU.mult),
                waits=[t6, ob_free[jb]],
                reads=["preb", "mu", "rstd"], writes=[f"ob{jb}"], inc="dve", settle=True)
            preb_free = t_on
            out_last = p.op("gpsimd", lambda g, j=j, jb=jb: g.dma_start(
                out=out_h[j * 128:(j + 1) * 128, :], in_=outb[jb][:]),
                waits=[t_on], inc="gv", inck=16)
            ob_free[jb] = out_last
        a2f_free[b] = last_mm_of_chunk

    fin = [(nm, p.semv[nm]) for nm in ("gy", "gaq", "gwq", "gv")]
    p.op("gpsimd", lambda g: g.tensor_copy(out=smc(C["dg2"]), in_=smc(C["dg1"])),
         waits=fin, inc="gp")
    return finish(nc, p)


def finish(nc, p):
    from contextlib import ExitStack
    with ExitStack() as cx:
        semh = {}
        for s in p.semv:
            semh[s] = cx.enter_context(nc.semaphore(s))
        block = cx.enter_context(nc.Block())
        p.replay(block, semh)
    return nc


_NC_CACHE = {}


def _get_nc(debug_pass1=False):
    key = bool(debug_pass1)
    if key not in _NC_CACHE:
        _NC_CACHE[key] = build(debug_pass1)
    return _NC_CACHE[key]


def _host_consts():
    ident = np.eye(128, dtype=np.float32)
    utri = np.triu(np.ones((128, 128), dtype=np.float32))
    idx = (np.arange(T, dtype=np.float64)[None, :] * 128
           + np.arange(128, dtype=np.float64)[:, None] + 1.0)
    invs = (1.0 / idx).astype(np.float32)
    return ident, utri, invs


def kernel(**inputs):
    _ = inputs["L0_attn_attention_mask"]
    hs = np.ascontiguousarray(np.asarray(inputs["L0_attn_hidden_states"],
                                         dtype=np.float32))
    gamma = np.asarray(inputs["L0_attn_gamma"], np.float32).reshape(-1)
    beta = np.asarray(inputs["L0_attn_beta"], np.float32).reshape(-1)
    ident, utri, invs = _host_consts()
    common = {
        "wq": np.ascontiguousarray(np.asarray(inputs["L0_attn_Wq"], np.float32)),
        "wk": np.ascontiguousarray(np.asarray(inputs["L0_attn_Wk"], np.float32)),
        "wv": np.ascontiguousarray(np.asarray(inputs["L0_attn_Wv"], np.float32)),
        "wqa": np.ascontiguousarray(np.asarray(inputs["L0_attn_Wqa"], np.float32)),
        "wka": np.ascontiguousarray(np.asarray(inputs["L0_attn_Wka"], np.float32)),
        "v_bq": np.ascontiguousarray(np.asarray(inputs["L0_attn_bq"], np.float32).reshape(1, D)),
        "v_bk": np.ascontiguousarray(np.asarray(inputs["L0_attn_bk"], np.float32).reshape(1, D)),
        "v_bv": np.ascontiguousarray(np.asarray(inputs["L0_attn_bv"], np.float32).reshape(1, D)),
        "v_bqa": np.ascontiguousarray(np.asarray(inputs["L0_attn_bqa"], np.float32).reshape(1, D)),
        "v_bka": np.ascontiguousarray(np.asarray(inputs["L0_attn_bka"], np.float32).reshape(1, D)),
        "gcolsc": np.ascontiguousarray(gamma.reshape(8, 128).T),
        "bcolsc": np.ascontiguousarray(beta.reshape(8, 128).T),
        "wgate": np.ascontiguousarray(np.asarray(inputs["L0_ffn_w_gate"], np.float32)),
        "wup": np.ascontiguousarray(np.asarray(inputs["L0_ffn_w_up"], np.float32)),
        "wdown": np.ascontiguousarray(np.asarray(inputs["L0_ffn_w_down"], np.float32)),
        "identc": ident, "utric": utri, "invsc": invs,
    }
    nc = _get_nc(False)
    in_maps = [dict(common, x=np.ascontiguousarray(hs[c])) for c in range(NC)]
    res = run_bass_kernel_spmd(nc, in_maps, core_ids=list(range(NC)))
    out = np.stack([np.asarray(r["out"], np.float32) for r in res.results], axis=0)
    return out
